# version 03
# baseline (speedup 1.0000x reference)
import contextlib
import math

import numpy as np
import ml_dtypes

import concourse.bass as bass
import concourse.mybir as mybir
from concourse.bass_utils import run_bass_kernel_spmd

F32 = mybir.dt.float32
BF16 = mybir.dt.bfloat16
AF = mybir.ActivationFunctionType
ALU = mybir.AluOpType
AX = mybir.AxisListType
BF = ml_dtypes.bfloat16

B, SEQ, D = 2, 4096, 4096
H, HD = 32, 128
NQ = 1024
QKV_O = 3 * D
HID, HIDP = 10928, 11264
H8, H8P, D8 = 1366, 1408, 512
EPS = 1e-5
QSCALE = 1.0 / math.sqrt(HD)
NW_ELEM = 8 * H8 * D8

ENGS = ("tensor", "vector", "scalar", "sync", "gpsimd")
SEMS = ("LD", "ST", "MM", "EV", "SC", "GP")

_TRIPLES = [(1, 2, 3), (1, 4, 5), (1, 7, 6), (2, 4, 6), (2, 5, 7), (3, 4, 7), (3, 6, 5)]


def _oct_maps():
    S = np.zeros((8, 8, 8), np.float32)
    S[0, 0, 0] = 1.0
    for i in range(1, 8):
        S[0, i, i] = 1.0
        S[i, 0, i] = 1.0
        S[i, i, 0] = -1.0
    for a, b, c in _TRIPLES:
        for x, y, z in ((a, b, c), (b, c, a), (c, a, b)):
            S[x, y, z] = 1.0
            S[y, x, z] = -1.0
    jm = np.zeros((8, 8), np.int32)
    sm = np.zeros((8, 8), np.float32)
    for i in range(8):
        for k in range(8):
            js = [j for j in range(8) if S[j, i, k] != 0]
            jm[i, k] = js[0]
            sm[i, k] = S[js[0], i, k]
    return jm, sm


JMAP, SMAP = _oct_maps()

NPH = 99
DEBUG = False
STT_MIN = True
SC_EVAC = True
ASTAGE = 99


class P:

    def __init__(self, nc, es):
        self.nc = nc
        self.q = {k: [] for k in ENGS}
        self.sem = {}
        self.cnt = {}
        for name in SEMS:
            self.sem[name] = es.enter_context(nc.semaphore(name))
            self.cnt[name] = 0
        self.land = None
        self.landi = 0
        self.flandi = 0

    def v(self, name):
        return self.cnt[name]

    def snap(self):
        return dict(self.cnt)

    def op(self, eng, fn, inc=None, k=1):
        if inc is None:
            self.q[eng].append(fn)
            return None
        sem = self.sem[inc]

        def g(e, fn=fn, sem=sem, k=k):
            fn(e).then_inc(sem, k)

        self.q[eng].append(g)
        self.cnt[inc] += k
        return self.cnt[inc]

    def wait(self, eng, name, val):
        if val <= 0:
            return
        sem = self.sem[name]
        self.q[eng].append(lambda e, sem=sem, val=val: e.wait_ge(sem, val))

    def wait_all(self, eng, snap):
        for name in SEMS:
            if snap[name] > 0:
                self.wait(eng, name, snap[name])

    def barrier_into(self, engs):
        s = self.snap()
        for eng in engs:
            self.wait_all(eng, s)


def build_program():
    nc = bass.Bass("TRN2", target_bir_lowering=False, debug=False)

    x_own = nc.dram_tensor("x_own", [NQ, D], F32, kind="ExternalInput").ap()
    x_bat = nc.dram_tensor("x_bat", [SEQ, D], F32, kind="ExternalInput").ap()
    w_qkv = nc.dram_tensor("w_qkv", [QKV_O, D], F32, kind="ExternalInput").ap()
    w_out = nc.dram_tensor("w_out", [D, D], F32, kind="ExternalInput").ap()
    bq_t = nc.dram_tensor("bq_t", [128, 96], F32, kind="ExternalInput").ap()
    bo_t = nc.dram_tensor("bo_t", [128, 32], F32, kind="ExternalInput").ap()
    wg_p = nc.dram_tensor("wg_p", [HIDP, D8], F32, kind="ExternalInput").ap()
    wu_p = nc.dram_tensor("wu_p", [HIDP, D8], F32, kind="ExternalInput").ap()
    wd_p = nc.dram_tensor("wd_p", [8 * D8, H8P], F32, kind="ExternalInput").ap()
    qbig_in = nc.dram_tensor("qbig", [128, 8], F32, kind="ExternalInput").ap()
    negk9_in = nc.dram_tensor("negk9", [1, SEQ], F32, kind="ExternalInput").ap()
    pos_in = nc.dram_tensor("pos", [1, SEQ], F32, kind="ExternalInput").ap()
    slopes_in = nc.dram_tensor("slopes", [128, H], F32, kind="ExternalInput").ap()

    y_out = nc.dram_tensor("y", [NQ, D], F32, kind="ExternalOutput").ap()

    wT_scr = nc.dram_tensor("wT_scr", [128 * 128, D], BF16, kind="Internal").ap()
    K_scr = nc.dram_tensor("K_scr", [H * 128, SEQ], BF16, kind="Internal").ap()
    V_scr = nc.dram_tensor("V_scr", [H * 128, SEQ], BF16, kind="Internal").ap()
    Q_scr = nc.dram_tensor("Q_scr", [H * 128, NQ], BF16, kind="Internal").ap()
    G_scr = nc.dram_tensor("G_scr", [HIDP, D8], BF16, kind="Internal").ap()
    U_scr = nc.dram_tensor("U_scr", [HIDP, D8], BF16, kind="Internal").ap()
    Dn_scr = nc.dram_tensor("Dn_scr", [8 * D8, H8P], BF16, kind="Internal").ap()
    x1_scr = nc.dram_tensor("x1_scr", [NQ, D], F32, kind="Internal").ap()
    ln2_scr = nc.dram_tensor("ln2_scr", [D, 512], BF16, kind="Internal").ap()

    ident_in = nc.inline_tensor(np.eye(128, dtype=BF), "identb").ap()
    identf_in = nc.inline_tensor(np.eye(128, dtype=np.float32), "identf").ap()
    onesf_in = nc.inline_tensor(np.ones((128, 1), dtype=np.float32), "onesf").ap()
    onesr_in = nc.inline_tensor(np.ones((1, 128), dtype=np.float32), "onesr").ap()

    es = contextlib.ExitStack()
    with es:
        p = P(nc, es)
        _sbctr = iter(range(100000))
        SB = lambda name, shape, dt: nc.sbuf_tensor(
            f"{name}_{next(_sbctr)}", shape, dt)
        stat = lambda name, shape, dt: es.enter_context(SB(name, shape, dt))

        ident_sb = stat("ident_sb", [128, 128], BF16)
        identf_sb = stat("identf_sb", [128, 128], F32)
        onesf_sb = stat("onesf_sb", [128, 1], F32)
        onesr_sb = stat("onesr_sb", [1, 128], F32)
        bq_sb = stat("bq_sb", [128, 96], F32)
        bo_sb = stat("bo_sb", [128, 32], F32)
        qbig_sb = stat("qbig_sb", [128, 8], F32)
        pos_sb = stat("pos_sb", [1, SEQ], F32)
        slopes_sb = stat("slopes_sb", [128, H], F32)
        thr_sb = stat("thr_sb", [128, 12], F32)
        lnst = stat("lnst", [128, 16], F32)
        sm_max = stat("sm_max", [128, 8], F32)
        sm_sum = stat("sm_sum", [128, 8], F32)
        sm_ln = stat("sm_ln", [128, 8], F32)
        sm_rec = stat("sm_rec", [128, 8], F32)
        sq_sb = stat("sq_sb", [128, D], F32)

        fsA = stat("fsA", [128, 512], F32)
        fsB = stat("fsB", [128, 512], F32)
        fsC = stat("fsC", [128, 512], F32)
        fsD = stat("fsD", [128, 512], F32)
        drn = stat("drn", [128, 2], F32)

        def dve_drain():
            g1 = p.op("vector", lambda e: e.tensor_copy(drn[:, 0:1], drn[:, 1:2]),
                      inc="EV")
            p.wait("vector", "EV", g1)
            g2 = p.op("vector", lambda e: e.tensor_copy(drn[:, 0:1], drn[:, 1:2]),
                      inc="EV")
            return g2

        def gp_drain():
            g1 = p.op("gpsimd", lambda e: e.tensor_copy(drn[:, 0:1], drn[:, 1:2]),
                      inc="GP")
            p.wait("gpsimd", "GP", g1)
            g2 = p.op("gpsimd", lambda e: e.tensor_copy(drn[:, 0:1], drn[:, 1:2]),
                      inc="GP")
            return g2

        psA = es.enter_context(nc.psum_tensor("psA", [128, 512], F32))
        psB = es.enter_context(nc.psum_tensor("psB", [128, 512], F32))
        psC = es.enter_context(nc.psum_tensor("psC", [128, 512], F32))
        psD = es.enter_context(nc.psum_tensor("psD", [128, 512], F32))
        psE = es.enter_context(nc.psum_tensor("psE", [128, 512], F32))
        psF = es.enter_context(nc.psum_tensor("psF", [128, 512], F32))
        psT = es.enter_context(nc.psum_tensor("psT", [128, 512], BF16))
        psU = es.enter_context(nc.psum_tensor("psU", [128, 512], BF16))

        p.land = [[psT, 0], [psU, 0]]
        p.fland = [[psE, 0], [psF, 0]]

        def tr_batch(srcs, dsts, f32=False, extra_src_wait=None):
            idt = identf_sb if f32 else ident_sb
            lands = p.fland if f32 else p.land
            gsz = 2 if f32 else 4
            n = len(srcs)
            gi = 0
            while gi * gsz < n:
                lo = gi * gsz
                hi = min(lo + gsz, n)
                land, free_ev = lands[gi % 2] if False else lands[(p.flandi if f32 else p.landi) % 2]
                p.wait("tensor", "EV", free_ev)
                if extra_src_wait is not None:
                    w = extra_src_wait(lo, hi)
                    if w:
                        p.wait("tensor", w[0], w[1])
                for j, si in enumerate(range(lo, hi)):
                    sbuf, scol = srcs[si]
                    p.op("tensor",
                         lambda e, land=land, j=j, sbuf=sbuf, scol=scol, idt=idt:
                         e.transpose(land[:, 128 * j:128 * (j + 1)],
                                     sbuf[:, scol:scol + 128], idt[:]),
                         inc="MM" if si == hi - 1 else None)
                mmv = p.v("MM")
                p.wait("vector", "MM", mmv)
                for j, si in enumerate(range(lo, hi)):
                    dbuf_, dcol = dsts[si]
                    p.op("vector",
                         lambda e, land=land, j=j, dbuf_=dbuf_, dcol=dcol:
                         e.tensor_copy(dbuf_[:, dcol:dcol + 128],
                                       land[:, 128 * j:128 * (j + 1)]),
                         inc="EV" if si == hi - 1 else None)
                if f32:
                    p.fland[p.flandi % 2][1] = p.v("EV")
                    p.flandi += 1
                else:
                    p.land[p.landi % 2][1] = p.v("EV")
                    p.landi += 1
                gi += 1

        p.op("vector", lambda e: e.memset(drn[:], 0.0), inc="EV")
        for out_t, in_t in (
            (ident_sb, ident_in), (identf_sb, identf_in), (onesf_sb, onesf_in),
            (onesr_sb, onesr_in), (bq_sb, bq_t), (bo_sb, bo_t),
            (qbig_sb, qbig_in), (pos_sb, pos_in), (slopes_sb, slopes_in),
        ):
            p.op("sync", lambda e, o=out_t, i=in_t: e.dma_start(out=o[:], in_=i[:, :]),
                 inc="LD", k=16)

        with SB("wsA", [128, D], BF16) as wsA, SB("wsB", [128, D], BF16) as wsB, \
             SB("wtA", [128, D], BF16) as wtA, SB("wtB", [128, D], BF16) as wtB, \
             SB("wfA", [128, D], F32) as wfA, SB("wfB", [128, D], F32) as wfB:
            p.barrier_into(("sync", "scalar"))
            mm_end = {}
            ev_end = {}
            st_end = {}
            ld_end = {}
            cast_sc = {}
            for oc in range(128):
                src = (w_qkv[128 * oc:128 * (oc + 1), :] if oc < 96 else
                       w_out[128 * (oc - 96):128 * (oc - 95), :])
                buf = wsA if oc % 2 == 0 else wsB
                wf = wfA if oc % 2 == 0 else wfB
                wt = wtA if oc % 2 == 0 else wtB
                if oc >= 2:
                    p.wait("sync", "SC", cast_sc[oc - 2])
                ld_end[oc] = p.op("sync",
                                  lambda e, wf=wf, src=src: e.dma_start(out=wf[:], in_=src),
                                  inc="LD", k=16)
                p.wait("scalar", "LD", ld_end[oc])
                if oc >= 2:
                    p.wait("scalar", "MM", mm_end[oc - 2])
                cast_sc[oc] = p.op("scalar", lambda e, wf=wf, buf=buf: e.activation(
                    out=buf[:], in_=wf[:], func=AF.Copy, bias=0.0, scale=1.0),
                    inc="SC")
                p.wait("tensor", "SC", cast_sc[oc])
                if oc >= 2:
                    p.wait("vector", "ST", st_end[oc - 2])
                tr_batch([(buf, 128 * ds) for ds in range(32)],
                         [(wt, 128 * ds) for ds in range(32)])
                mm_end[oc] = p.v("MM")
                ev_end[oc] = dve_drain()
                p.wait("scalar", "EV", ev_end[oc])
                st_end[oc] = p.op("scalar",
                                  lambda e, oc=oc, wt=wt: e.dma_start(
                                      out=wT_scr[128 * oc:128 * (oc + 1), :], in_=wt[:]),
                                  inc="ST", k=16)

            def dequant(tag, w_in, scr, n_rt, ncols, n_tr, col):
                p.barrier_into(ENGS)
                with SB("acc" + tag, [128, n_rt], F32) as acc, \
                     SB("qa" + tag, [128, ncols], BF16) as qA, \
                     SB("qb" + tag, [128, ncols], BF16) as qB, \
                     SB("sa" + tag, [128, ncols], BF16) as sA, \
                     SB("sb" + tag, [128, ncols], BF16) as sB, \
                     SB("f0" + tag, [128, ncols], F32) as f0, \
                     SB("f1" + tag, [128, ncols], F32) as f1, \
                     SB("t0" + tag, [128, ncols], F32) as t0, \
                     SB("t1" + tag, [128, ncols], F32) as t1, \
                     SB("u0" + tag, [128, ncols], F32) as u0, \
                     SB("u1" + tag, [128, ncols], F32) as u1:
                    sc_end = {}
                    for r in range(n_rt):
                        buf = f0 if r % 2 == 0 else f1
                        if r >= 2:
                            p.wait("sync", "SC", sc_end[r - 2])
                        ldv = p.op("sync", lambda e, buf=buf, r=r: e.dma_start(
                            out=buf[:], in_=w_in[128 * r:128 * (r + 1), :]),
                            inc="LD", k=16)
                        p.wait("scalar", "LD", ldv)
                        sc_end[r] = p.op("scalar", lambda e, buf=buf, r=r: e.activation(
                            out=sq_sb[:, 0:ncols], in_=buf[:], func=AF.Abs,
                            accum_out=acc[:, r:r + 1]), inc="SC")
                    p.wait("vector", "SC", sc_end[n_rt - 1])
                    ev1 = p.op("vector", lambda e: e.reduce_sum(
                        fsA[:, 0:1], acc[:, 0:n_rt], axis=AX.X), inc="EV")
                    p.wait("tensor", "EV", ev1)
                    p.wait("tensor", "EV", p.fland[0][1])
                    mm1 = p.op("tensor", lambda e: e.matmul(
                        psE[0:1, 0:1], fsA[:, 0:1], onesf_sb[:],
                        start=True, stop=True), inc="MM")
                    p.wait("vector", "MM", mm1)
                    p.op("vector", lambda e: e.tensor_scalar_mul(
                        fsB[0:1, 0:1], psE[0:1, 0:1], 0.5 / NW_ELEM))
                    ev2 = p.op("vector", lambda e: e.tensor_scalar_mul(
                        fsB[0:1, 1:2], psE[0:1, 0:1], 1.0 / NW_ELEM), inc="EV")
                    p.wait("tensor", "EV", ev2)
                    mm2 = p.op("tensor", lambda e: e.matmul(
                        psE[:, 0:2], onesr_sb[:], fsB[0:1, 0:2],
                        start=True, stop=True), inc="MM")
                    p.wait("vector", "MM", mm2)
                    p.op("vector", lambda e, col=col: e.tensor_copy(
                        thr_sb[:, col:col + 1], psE[:, 0:1]))
                    p.op("vector", lambda e, col=col: e.tensor_scalar_mul(
                        thr_sb[:, col + 3:col + 4], psE[:, 0:1], -1.0))
                    ev3 = p.op("vector", lambda e, col=col: e.tensor_copy(
                        thr_sb[:, col + 6:col + 7], psE[:, 1:2]), inc="EV")
                    p.fland[0][1] = ev3

                    tern_gp = {}
                    st_end2 = {}
                    mm_end2 = {}
                    cmp_ev = {}
                    for r in range(n_rt):
                        buf = f0 if r % 2 == 0 else f1
                        qt = qA if r % 2 == 0 else qB
                        stg = sA if r % 2 == 0 else sB
                        ge_ = t0 if r % 2 == 0 else t1
                        le_ = u0 if r % 2 == 0 else u1
                        if r >= 2:
                            p.wait("sync", "GP", tern_gp[r - 2])
                        else:
                            p.wait("sync", "SC", sc_end[n_rt - 1])
                        ldv = p.op("sync", lambda e, buf=buf, r=r: e.dma_start(
                            out=buf[:], in_=w_in[128 * r:128 * (r + 1), :]),
                            inc="LD", k=16)
                        p.wait("vector", "LD", ldv)
                        p.wait("vector", "EV", ev3)
                        if r >= 2:
                            p.wait("vector", "GP", tern_gp[r - 2])
                        p.op("vector", lambda e, buf=buf, col=col, ge_=ge_:
                             e.tensor_scalar(
                                 out=ge_[:], in0=buf[:],
                                 scalar1=thr_sb[:, col:col + 1],
                                 scalar2=None, op0=ALU.is_ge))
                        cmp_ev[r] = p.op("vector", lambda e, buf=buf, col=col,
                                         le_=le_: e.tensor_scalar(
                            out=le_[:], in0=buf[:],
                            scalar1=thr_sb[:, col + 3:col + 4],
                            scalar2=None, op0=ALU.is_le), inc="EV")
                        p.wait("gpsimd", "EV", cmp_ev[r])
                        if r >= 2:
                            p.wait("gpsimd", "MM", mm_end2[r - 2])
                        tern_gp[r] = p.op("gpsimd", lambda e, qt=qt, ge_=ge_, le_=le_:
                                          e.tensor_sub(qt[:], ge_[:], le_[:]),
                                          inc="GP")
                        tern_gp[r] = gp_drain()
                        p.wait("tensor", "GP", tern_gp[r])
                        if r >= 2:
                            p.wait("vector", "ST", st_end2[r - 2])
                        tr_batch([(qt, 128 * t) for t in range(n_tr)],
                                 [(stg, 128 * t) for t in range(n_tr)])
                        mm_end2[r] = p.v("MM")
                        p.wait("scalar", "EV", dve_drain())
                        st_end2[r] = p.op("scalar", lambda e, r=r, stg=stg: e.dma_start(
                            out=scr[128 * r:128 * (r + 1), :], in_=stg[:]),
                            inc="ST", k=16)

            if NPH >= 2:
                dequant("g", wg_p, G_scr, HIDP // 128, D8, 4, 0)
                dequant("u", wu_p, U_scr, HIDP // 128, D8, 4, 1)
                dequant("d", wd_p, Dn_scr, 8 * D8 // 128, H8P, 11, 2)

            p.op("vector", lambda e: e.tensor_mul(
                thr_sb[:, 9:10], thr_sb[:, 7:8], thr_sb[:, 8:9]), inc="EV")

        def ln_tiles(tiles, outs, ld_gate):
            p.wait("scalar", "LD", ld_gate)
            for ti, (xbuf, sl) in enumerate(tiles):
                p.op("scalar", lambda e, xbuf=xbuf, sl=sl: e.activation(
                    out=sq_sb[:], in_=xbuf[:], func=AF.Square,
                    accum_out=lnst[:, 8 * sl + 7:8 * sl + 8]),
                    inc="SC" if ti == len(tiles) - 1 else None)
            sc1 = p.v("SC")
            p.wait("vector", "LD", ld_gate)
            for xbuf, sl in tiles:
                p.op("vector", lambda e, xbuf=xbuf, sl=sl: e.reduce_sum(
                    lnst[:, 8 * sl:8 * sl + 1], xbuf[:], axis=AX.X))
            for xbuf, sl in tiles:
                p.op("vector", lambda e, sl=sl: e.tensor_scalar_mul(
                    lnst[:, 8 * sl + 1:8 * sl + 2], lnst[:, 8 * sl:8 * sl + 1],
                    -1.0 / D))
            for xbuf, sl in tiles:
                p.op("vector", lambda e, sl=sl: e.tensor_mul(
                    lnst[:, 8 * sl + 2:8 * sl + 3], lnst[:, 8 * sl + 1:8 * sl + 2],
                    lnst[:, 8 * sl + 1:8 * sl + 2]))
            p.wait("vector", "SC", sc1)
            for xbuf, sl in tiles:
                p.op("vector", lambda e, sl=sl: e.tensor_scalar(
                    out=lnst[:, 8 * sl + 3:8 * sl + 4],
                    in0=lnst[:, 8 * sl + 7:8 * sl + 8],
                    scalar1=1.0 / D, scalar2=EPS, op0=ALU.mult, op1=ALU.add))
            for ti, (xbuf, sl) in enumerate(tiles):
                p.op("vector", lambda e, sl=sl: e.tensor_sub(
                    lnst[:, 8 * sl + 4:8 * sl + 5], lnst[:, 8 * sl + 3:8 * sl + 4],
                    lnst[:, 8 * sl + 2:8 * sl + 3]),
                    inc="EV" if ti == len(tiles) - 1 else None)
            ev1 = p.v("EV")
            p.wait("scalar", "EV", ev1)
            for xbuf, sl in tiles:
                p.op("scalar", lambda e, sl=sl: e.activation(
                    out=lnst[:, 8 * sl + 5:8 * sl + 6],
                    in_=lnst[:, 8 * sl + 4:8 * sl + 5], func=AF.Ln))
            for ti, (xbuf, sl) in enumerate(tiles):
                p.op("scalar", lambda e, sl=sl: e.activation(
                    out=lnst[:, 8 * sl + 6:8 * sl + 7],
                    in_=lnst[:, 8 * sl + 5:8 * sl + 6], func=AF.Exp, scale=-0.5),
                    inc="SC" if ti == len(tiles) - 1 else None)
            sc2 = p.v("SC")
            p.wait("vector", "SC", sc2)
            for ti, ((xbuf, sl), ob) in enumerate(zip(tiles, outs)):
                p.op("vector", lambda e, xbuf=xbuf, sl=sl, ob=ob: e.tensor_scalar(
                    out=ob[:], in0=xbuf[:],
                    scalar1=lnst[:, 8 * sl + 1:8 * sl + 2],
                    scalar2=lnst[:, 8 * sl + 6:8 * sl + 7],
                    op0=ALU.add, op1=ALU.mult),
                    inc="EV" if ti == len(tiles) - 1 else None)
            return p.v("EV")

        with SB("fmA3", [128, 32 * 1024], BF16) as fmA, \
             SB("evb0", [128, 512], BF16) as evb0, \
             SB("evb1", [128, 512], BF16) as evb1, \
             SB("evb2", [128, 512], BF16) as evb2, \
             SB("evb3", [128, 512], BF16) as evb3, \
             SB("xa3", [128, D], F32) as xa, \
             SB("xb3", [128, D], F32) as xb, \
             SB("lbA", [128, D], BF16) as lbA, \
             SB("lbB", [128, D], BF16) as lbB, \
             SB("wsA3", [128, D], BF16) as wsA, \
             SB("wsB3", [128, D], BF16) as wsB, \
             SB("vsA", [128, 512], BF16) as vsA, \
             SB("vsB", [128, 512], BF16) as vsB:

            def proj_pass(x_src, tb, oc_list):
                p.barrier_into(ENGS)
                norm_ev = {}
                mm_tr = {}
                for pr in range(4):
                    if pr >= 1:
                        p.wait("sync", "EV", norm_ev[pr - 1])
                    for half in range(2):
                        tt = 2 * pr + half
                        buf = xa if half == 0 else xb
                        ldv = p.op("sync", lambda e, buf=buf, tt=tt: e.dma_start(
                            out=buf[:], in_=x_src[128 * tt:128 * (tt + 1), :]),
                            inc="LD", k=16)
                    if pr >= 1:
                        p.wait("vector", "MM", mm_tr[pr - 1])
                    norm_ev[pr] = ln_tiles([(xa, 0), (xb, 1)], [lbA, lbB], ldv)
                    for half in range(2):
                        tt = 2 * pr + half
                        lb = lbA if half == 0 else lbB
                        p.wait("tensor", "EV", norm_ev[pr])
                        tr_batch([(lb, 128 * ds) for ds in range(32)],
                                 [(fmA, 1024 * ds + 128 * tt) for ds in range(32)])
                    mm_tr[pr] = p.v("MM")
                ev_ln = dve_drain()

                mm_end = {}
                ev_end = {}
                st_end = {}
                p.wait("tensor", "EV", ev_ln)
                for idx, oc in enumerate(oc_list):
                    is_v = 64 <= oc < 96
                    buf = wsA if idx % 2 == 0 else wsB
                    ev0_ = (evb0, evb2)[idx % 2]
                    ev1_ = (evb1, evb3)[idx % 2]
                    ps0 = psA if idx % 2 == 0 else psC
                    ps1 = psB if idx % 2 == 0 else psD
                    if idx >= 2:
                        p.wait("sync", "MM", mm_end[idx - 2])
                    ldv = p.op("sync", lambda e, buf=buf, oc=oc: e.dma_start(
                        out=buf[:], in_=wT_scr[128 * oc:128 * (oc + 1), :]),
                        inc="LD", k=16)
                    p.wait("tensor", "LD", ldv)
                    if idx >= 2:
                        p.wait("tensor", "EV", ev_end[idx - 2])
                    for ds in range(32):
                        for strip, ps in ((0, ps0), (1, ps1)):
                            p.op("tensor", lambda e, buf=buf, ds=ds, strip=strip, ps=ps:
                                 e.matmul(ps[:],
                                          buf[:, 128 * ds:128 * (ds + 1)],
                                          fmA[:, 1024 * ds + 512 * strip:
                                              1024 * ds + 512 * strip + 512],
                                          start=(ds == 0), stop=(ds == 31)),
                                 inc="MM" if (ds == 31) else None)
                    mm_end[idx] = p.v("MM")
                    p.wait("vector", "MM", mm_end[idx])
                    if idx >= 2:
                        p.wait("vector", "ST", st_end[idx - 2])
                    for strip, ps, eb in ((0, ps0, ev0_), (1, ps1, ev1_)):
                        if oc < 32:
                            p.op("vector", lambda e, ps=ps, eb=eb, oc=oc: e.tensor_scalar(
                                out=eb[:], in0=ps[:], scalar1=bq_sb[:, oc:oc + 1],
                                scalar2=QSCALE, op0=ALU.add, op1=ALU.mult),
                                inc="EV")
                        else:
                            p.op("vector", lambda e, ps=ps, eb=eb, oc=oc: e.tensor_scalar(
                                out=eb[:], in0=ps[:], scalar1=bq_sb[:, oc:oc + 1],
                                scalar2=None, op0=ALU.add), inc="EV")
                    ev_end[idx] = p.v("EV")
                    if not is_v:
                        evd = dve_drain()
                        for strip, eb in ((0, ev0_), (1, ev1_)):
                            if oc < 32:
                                dst = Q_scr[128 * oc:128 * (oc + 1),
                                            512 * strip:512 * (strip + 1)]
                            else:
                                h = oc - 32
                                c0 = 1024 * tb + 512 * strip
                                dst = K_scr[128 * h:128 * (h + 1), c0:c0 + 512]
                            p.wait("scalar", "EV", evd)
                            p.op("scalar", lambda e, dst=dst, eb=eb: e.dma_start(
                                out=dst, in_=eb[:]), inc="ST", k=16)
                    else:
                        h = oc - 64
                        evd0 = dve_drain()
                        for strip, eb, vs in ((0, ev0_, vsA), (1, ev1_, vsB)):
                            if idx >= 1 and oc > 64:
                                p.wait("vector", "ST", st_end[idx - 1])
                            p.wait("tensor", "EV", evd0)
                            tr_batch([(eb, 128 * t) for t in range(4)],
                                     [(vs, 128 * t) for t in range(4)])
                            evv = dve_drain()
                            c0 = (8 * tb + 4 * strip) * 128
                            p.wait("scalar", "EV", evv)
                            p.op("scalar", lambda e, h=h, c0=c0, vs=vs: e.dma_start(
                                out=V_scr[128 * h:128 * (h + 1), c0:c0 + 512],
                                in_=vs[:]), inc="ST", k=16)
                    st_end[idx] = p.v("ST")
                    ev_end[idx] = p.v("EV")
                    mm_end[idx] = p.v("MM")

            if NPH >= 3:
                for tb in range(4):
                    proj_pass(x_bat[1024 * tb:1024 * (tb + 1), :], tb,
                              list(range(32, 96)))
                proj_pass(x_own[:, :], 0, list(range(0, 32)))

        for chunk in range(2 if NPH >= 4 else 0):
            g_fmB = SB("fmB", [128, 32 * 512], BF16)
            fmB = g_fmB.__enter__()
            g_ph4 = [SB("fmA4", [128, 4 * SEQ + 16 * 512], BF16),
                     SB("alibi_s", [128, SEQ], F32),
                     SB("posbc_s", [128, SEQ], F32),
                     SB("negk9_s", [128, SEQ], F32),
                     SB("Ksb", [128, SEQ], BF16),
                     SB("Vsb", [128, SEQ], BF16),
                     SB("Qsb", [128, 512], BF16),
                     SB("xa4", [128, D], F32)]
            (fmA4, alibi_sb, posbc_sb, negk9_sb, Ksb, Vsb, Qsb, xa) = [
                g.__enter__() for g in g_ph4]
            if True:
                p.barrier_into(("sync",))
                src = bass.AP(negk9_in.tensor, 0, [[0, 128], [1, SEQ]])
                p.op("sync", lambda e, src=src: e.dma_start(out=negk9_sb[:], in_=src),
                     inc="LD", k=16)
                src2 = bass.AP(pos_in.tensor, 0, [[0, 128], [1, SEQ]])
                p.op("sync", lambda e, src2=src2: e.dma_start(out=posbc_sb[:], in_=src2),
                     inc="LD", k=16)
                p.barrier_into(ENGS)

                head_mm = {}
                head_gp = {}
                head_ev = {}
                head_sc = {}
                head_evread = {}
                for h in range(H):
                    if h >= 1:
                        p.wait("sync", "MM", head_mm[h - 1])
                        p.wait("sync", "EV", head_ev[h - 1])
                    ldk = p.op("sync", lambda e, h=h: e.dma_start(
                        out=Ksb[:], in_=K_scr[128 * h:128 * (h + 1), :]),
                        inc="LD", k=16)
                    ldv = p.op("sync", lambda e, h=h: e.dma_start(
                        out=Vsb[:], in_=V_scr[128 * h:128 * (h + 1), :]),
                        inc="LD", k=16)
                    ldq = p.op("sync", lambda e, h=h, chunk=chunk: e.dma_start(
                        out=Qsb[:], in_=Q_scr[128 * h:128 * (h + 1),
                                              512 * chunk:512 * (chunk + 1)]),
                        inc="LD", k=16)
                    if ASTAGE >= 2:
                        if h >= 1:
                            p.wait("gpsimd", "EV", head_evread[h - 1])
                        p.op("gpsimd", lambda e, h=h: e.tensor_scalar_mul(
                            alibi_sb[:], posbc_sb[:], slopes_sb[:, h:h + 1]),
                            inc="GP")
                        alibi_gp = gp_drain()

                    stt_end = {}
                    gp_end = {}
                    sc_norm = {}
                    p.wait("tensor", "LD", ldq)
                    for qt in range(4 if ASTAGE >= 3 else 0):
                        qtg = 4 * chunk + qt
                        for kc in range(8):
                            ki = qt * 8 + kc
                            ps = (psA, psB, psC, psD)[ki % 4]
                            if SC_EVAC and ki >= 4:
                                p.wait("tensor", "EV", stt_end[ki - 4])
                            p.op("tensor", lambda e, qt=qt, kc=kc, ps=ps: e.matmul(
                                ps[:], Qsb[:, 128 * qt:128 * (qt + 1)],
                                Ksb[:, 512 * kc:512 * (kc + 1)],
                                start=True, stop=True), inc="MM")
                            if not SC_EVAC:
                                stt_end[ki] = p.v("EV")
                                continue
                            p.wait("vector", "MM", p.v("MM"))
                            if kc == 0:
                                if qt >= 1:
                                    p.wait("vector", "SC", sc_norm[qt - 1] - 3)
                                elif h >= 1:
                                    p.wait("vector", "SC", head_sc[h - 1] - 3)
                                if qt == 0:
                                    p.wait("vector", "GP", alibi_gp)
                            fs_ = fsA if kc % 2 == 0 else fsB
                            p.op("vector",
                                 lambda e, fs_=fs_, kc=kc, ps=ps, qtg=qtg:
                                 e.scalar_tensor_tensor(
                                     out=fs_[:],
                                     in0=negk9_sb[:, 512 * kc:512 * (kc + 1)],
                                     scalar=qbig_sb[:, qtg:qtg + 1],
                                     in1=ps[:], op0=ALU.add, op1=ALU.min))
                            stt_end[ki] = p.op(
                                "vector",
                                lambda e, fs_=fs_, kc=kc: e.tensor_tensor(
                                    out=sq_sb[:, 512 * kc:512 * (kc + 1)],
                                    in0=fs_[:],
                                    in1=alibi_sb[:, 512 * kc:512 * (kc + 1)],
                                    op=ALU.add),
                                inc="EV")
                        if ASTAGE < 4:
                            gp_end[qt] = p.v("GP")
                            sc_norm[qt] = p.v("SC") + 3
                            continue
                        p.op("vector", lambda e: e.tensor_copy(drn[:, 0:1],
                                                               drn[:, 1:2]))
                        evm = p.op("vector", lambda e, qt=qt: e.reduce_max(
                            sm_max[:, qt:qt + 1], sq_sb[:], axis=AX.X, negate=True),
                            inc="EV")
                        gp_end[qt] = p.v("GP")
                        p.wait("scalar", "EV", evm)
                        s1 = p.op("scalar", lambda e, qt=qt: e.activation(
                            out=xa[:], in_=sq_sb[:], func=AF.Exp,
                            bias=sm_max[:, qt:qt + 1],
                            accum_out=sm_sum[:, qt:qt + 1]), inc="SC")
                        p.wait("scalar", "SC", s1)
                        s2 = p.op("scalar", lambda e, qt=qt: e.activation(
                            out=sm_ln[:, qt:qt + 1], in_=sm_sum[:, qt:qt + 1],
                            func=AF.Ln), inc="SC")
                        p.wait("scalar", "SC", s2)
                        s3 = p.op("scalar", lambda e, qt=qt: e.activation(
                            out=sm_rec[:, qt:qt + 1], in_=sm_ln[:, qt:qt + 1],
                            func=AF.Exp, scale=-1.0), inc="SC")
                        p.wait("scalar", "SC", s3)
                        sc_norm[qt] = p.op("scalar", lambda e, qt=qt: e.activation(
                            out=fmA4[:, SEQ * qt:SEQ * (qt + 1)], in_=xa[:],
                            func=AF.Copy, bias=0.0,
                            scale=sm_rec[:, qt:qt + 1]), inc="SC")

                    if ASTAGE >= 4:
                        p.wait("tensor", "SC", sc_norm[3])
                    p.wait("tensor", "LD", ldv)
                    av_mm = {}
                    for khalf in range(2 if ASTAGE >= 5 else 0):
                        pt_ev = {}
                        for kt in range(16):
                            ktg = 16 * khalf + kt
                            if khalf == 1:
                                p.wait("vector", "MM", av_mm[kt])
                            tr_batch(
                                [(fmA4, SEQ * qt + 128 * ktg) for qt in range(4)],
                                [(fmA4, 4 * SEQ + 512 * kt + 128 * qt)
                                 for qt in range(4)])
                            pt_ev[kt] = p.v("EV")
                        kh_drain = dve_drain()
                        for kt in range(16):
                            pt_ev[kt] = kh_drain
                        for kt in range(16 if ASTAGE >= 6 else 0):
                            ktg = 16 * khalf + kt
                            if khalf == 0 and kt == 0 and h >= 1:
                                p.wait("tensor", "EV", head_ev[h - 1])
                            p.wait("tensor", "EV", pt_ev[kt])
                            avv = p.op("tensor", lambda e, ktg=ktg, kt=kt, khalf=khalf:
                                       e.matmul(
                                           psE[:], Vsb[:, 128 * ktg:128 * (ktg + 1)],
                                           fmA4[:, 4 * SEQ + 512 * kt:
                                                4 * SEQ + 512 * (kt + 1)],
                                           start=(khalf == 0 and kt == 0),
                                           stop=(khalf == 1 and kt == 15)), inc="MM")
                            if khalf == 0:
                                av_mm[kt] = avv
                    head_mm[h] = p.v("MM")
                    if ASTAGE >= 6:
                        p.wait("vector", "MM", head_mm[h])
                        p.op("vector", lambda e, h=h: e.tensor_copy(
                            fmB[:, 512 * h:512 * (h + 1)], psE[:]), inc="EV")
                        head_ev[h] = dve_drain()
                        p.fland[0][1] = head_ev[h]
                    else:
                        head_ev[h] = p.v("EV")
                    head_gp[h] = gp_end.get(3, p.v("GP"))
                    head_sc[h] = sc_norm.get(3, p.v("SC") + 3)
                    head_evread[h] = stt_end.get(31, p.v("EV"))

                for g in reversed(g_ph4):
                    g.__exit__(None, None, None)

                if NPH < 5:
                    g_fmB.__exit__(None, None, None)
                    continue
                with SB("ws5A", [128, D], BF16) as wsA5, \
                     SB("ws5B", [128, D], BF16) as wsB5, \
                     SB("l2st", [128, 32 * 256], BF16) as l2st:
                    for hf in range(2):
                        with SB("xo0", [128, D], F32) as xo0, \
                             SB("xo1", [128, D], F32) as xo1, \
                             SB("x10", [128, D], F32) as x10, \
                             SB("x11", [128, D], F32) as x11, \
                             SB("lb20", [128, D], BF16) as lb20, \
                             SB("lb21", [128, D], BF16) as lb21:
                            p.barrier_into(ENGS)
                            r0 = 512 * chunk + 256 * hf
                            ldx = {}
                            for tt, xo in ((0, xo0), (1, xo1)):
                                ldx[tt] = p.op("sync", lambda e, r0=r0, tt=tt, xo=xo:
                                               e.dma_start(out=xo[:],
                                                           in_=x_own[r0 + 128 * tt:
                                                                     r0 + 128 * (tt + 1), :]),
                                               inc="LD", k=16)
                            mm_end = {}
                            ev_end = {}
                            for oc in range(32):
                                buf = wsA5 if oc % 2 == 0 else wsB5
                                ps = psA if oc % 2 == 0 else psB
                                if oc >= 2:
                                    p.wait("sync", "MM", mm_end[oc - 2])
                                ldv = p.op("sync", lambda e, buf=buf, oc=oc: e.dma_start(
                                    out=buf[:],
                                    in_=wT_scr[128 * (96 + oc):128 * (97 + oc), :]),
                                    inc="LD", k=16)
                                p.wait("tensor", "LD", ldv)
                                if oc >= 2:
                                    p.wait("tensor", "EV", ev_end[oc - 2])
                                for ds in range(32):
                                    p.op("tensor", lambda e, buf=buf, ds=ds, ps=ps, hf=hf:
                                         e.matmul(ps[:, 0:256],
                                                  buf[:, 128 * ds:128 * (ds + 1)],
                                                  fmB[:, 512 * ds + 256 * hf:
                                                      512 * ds + 256 * hf + 256],
                                                  start=(ds == 0), stop=(ds == 31)),
                                         inc="MM" if ds == 31 else None)
                                mm_end[oc] = p.v("MM")
                                p.wait("vector", "MM", mm_end[oc])
                                p.op("vector", lambda e, ps=ps, oc=oc:
                                     e.tensor_scalar(
                                         out=fsA[:, 0:256], in0=ps[:, 0:256],
                                         scalar1=bo_sb[:, oc:oc + 1],
                                         scalar2=None, op0=ALU.add),
                                     inc="EV")
                                ev_end[oc] = dve_drain()
                                for tt, xo, x1 in ((0, xo0, x10), (1, xo1, x11)):
                                    land, free_ev = p.fland[p.flandi % 2]
                                    p.wait("tensor", "EV", max(free_ev, ev_end[oc]))
                                    p.op("tensor", lambda e, land=land, tt=tt:
                                         e.transpose(land[:, 0:128],
                                                     fsA[:, 128 * tt:128 * (tt + 1)],
                                                     identf_sb[:]), inc="MM")
                                    mmv = p.v("MM")
                                    p.wait("vector", "MM", mmv)
                                    p.wait("vector", "LD", ldx[tt])
                                    p.op("vector", lambda e, land=land, x1=x1, xo=xo, oc=oc:
                                         e.tensor_tensor(
                                             out=x1[:, 128 * oc:128 * (oc + 1)],
                                             in0=land[:, 0:128],
                                             in1=xo[:, 128 * oc:128 * (oc + 1)],
                                             op=ALU.add), inc="EV")
                                    p.fland[p.flandi % 2][1] = p.v("EV")
                                    p.flandi += 1
                                ev_end[oc] = p.v("EV")
                                mm_end[oc] = p.v("MM")
                            gv = p.op("vector", lambda e: e.tensor_copy(
                                lnst[:, 15:16], lnst[:, 14:15]), inc="EV")
                            p.wait("vector", "EV", gv)
                            gv2 = p.op("vector", lambda e: e.tensor_copy(
                                lnst[:, 15:16], lnst[:, 14:15]), inc="EV")
                            p.wait("scalar", "EV", gv2)
                            ev_n = ln_tiles([(x10, 0), (x11, 1)], [lb20, lb21], 0)
                            for tt, lb in ((0, lb20), (1, lb21)):
                                p.wait("tensor", "EV", ev_n)
                                tr_batch([(lb, 128 * ds) for ds in range(32)],
                                         [(l2st, 256 * ds + 128 * tt)
                                          for ds in range(32)])
                            ev_tr = dve_drain()
                            p.wait("scalar", "EV", ev_n)
                            for tt, x1 in ((0, x10), (1, x11)):
                                p.op("scalar", lambda e, r0=r0, tt=tt, x1=x1: e.dma_start(
                                    out=x1_scr[r0 + 128 * tt:r0 + 128 * (tt + 1), :],
                                    in_=x1[:]), inc="ST", k=16)
                            p.wait("scalar", "EV", ev_tr)
                            for ds in range(32):
                                p.op("scalar", lambda e, ds=ds, hf=hf: e.dma_start(
                                    out=ln2_scr[128 * ds:128 * (ds + 1),
                                                256 * hf:256 * (hf + 1)],
                                    in_=l2st[:, 256 * ds:256 * (ds + 1)]),
                                    inc="ST", k=16)
                            p.barrier_into(ENGS)

            g_fmB.__exit__(None, None, None)

            if NPH < 6:
                continue
            with SB("hfm", [128, 88 * 512], BF16) as hfm:
              with SB("fmL", [128, 32 * 512], BF16) as fmL, \
                 SB("gb0", [128, 8 * 512], BF16) as gb0, \
                 SB("gb1", [128, 8 * 512], BF16) as gb1, \
                 SB("ub0", [128, 8 * 512], BF16) as ub0, \
                 SB("ub1", [128, 8 * 512], BF16) as ub1:
                p.barrier_into(("sync",))
                ln2_src = bass.AP(ln2_scr.tensor, 0,
                                  [[512, 128], [512 * 128, 32], [1, 512]])
                ldl = p.op("sync", lambda e, s=ln2_src: e.dma_start(out=fmL[:], in_=s),
                           inc="LD", k=16)
                p.barrier_into(ENGS)

                ev_h = {}
                ev_sub = {}
                mm_h = {}
                sc_h = {}
                for hoc in range(88):
                    k = hoc // 11
                    hl = hoc % 11
                    gb = gb0 if hoc % 2 == 0 else gb1
                    ub = ub0 if hoc % 2 == 0 else ub1
                    if hoc >= 2:
                        p.wait("sync", "MM", mm_h[hoc - 2])
                    g_src = bass.AP(G_scr.tensor, (hl * 128) * D8,
                                    [[D8, 128], [11 * 128 * D8, 8], [1, D8]])
                    u_src = bass.AP(U_scr.tensor, (hl * 128) * D8,
                                    [[D8, 128], [11 * 128 * D8, 8], [1, D8]])
                    p.op("sync", lambda e, gb=gb, g_src=g_src: e.dma_start(
                        out=gb[:], in_=g_src), inc="LD", k=16)
                    ld_all = p.op("sync", lambda e, ub=ub, u_src=u_src: e.dma_start(
                        out=ub[:], in_=u_src), inc="LD", k=16)
                    ld_i = {i: ld_all for i in range(8)}
                    sgn = [SMAP[i, k] for i in range(8)]
                    first = {1.0: None, -1.0: None}
                    last = {1.0: None, -1.0: None}
                    for i in range(8):
                        if first[sgn[i]] is None:
                            first[sgn[i]] = i
                        last[sgn[i]] = i
                    if hoc >= 1:
                        p.wait("tensor", "EV", ev_sub[hoc - 1])
                    p.wait("tensor", "LD", ld_all)
                    for i in range(8):
                        j = int(JMAP[i, k])
                        gps = psA if sgn[i] > 0 else psB
                        ups = psC if sgn[i] > 0 else psD
                        st_ = (first[sgn[i]] == i)
                        sp_ = (last[sgn[i]] == i)
                        for ds in range(4):
                            co = 512 * j + 128 * ds
                            p.op("tensor", lambda e, gb=gb, co=co, ds=ds, gps=gps, i=i,
                                 st_=st_, sp_=sp_: e.matmul(
                                     gps[:], gb[:, co:co + 128],
                                     fmL[:, 512 * (4 * i + ds):512 * (4 * i + ds) + 512],
                                     start=(st_ and ds == 0), stop=(sp_ and ds == 3)),
                                 inc=None)
                            p.op("tensor", lambda e, ub=ub, co=co, ds=ds, ups=ups, i=i,
                                 st_=st_, sp_=sp_: e.matmul(
                                     ups[:], ub[:, co:co + 128],
                                     fmL[:, 512 * (4 * i + ds):512 * (4 * i + ds) + 512],
                                     start=(st_ and ds == 0), stop=(sp_ and ds == 3)),
                                 inc="MM" if (i == 7 and ds == 3) else None)
                    mm_h[hoc] = p.v("MM")
                    p.wait("scalar", "MM", mm_h[hoc])
                    if hoc >= 1:
                        p.wait("scalar", "EV", ev_h[hoc - 1])
                    p.op("scalar", lambda e: e.activation(
                        out=fsA[:], in_=psB[:], func=AF.Copy, bias=0.0, scale=-1.0))
                    scn = p.op("scalar", lambda e: e.activation(
                        out=fsB[:], in_=psD[:], func=AF.Copy, bias=0.0, scale=-1.0),
                        inc="SC")
                    p.wait("vector", "MM", mm_h[hoc])
                    p.wait("vector", "SC", scn)
                    if hoc >= 1:
                        p.wait("vector", "EV", ev_h[hoc - 1])
                    p.op("vector", lambda e: e.tensor_add(fsC[:], psA[:], fsA[:]))
                    ev_sub[hoc] = p.op("vector", lambda e: e.tensor_add(
                        fsD[:], psC[:], fsB[:]), inc="EV")
                    p.wait("scalar", "EV", ev_sub[hoc])
                    sc_h[hoc] = p.op("scalar", lambda e: e.activation(
                        out=fsA[:], in_=fsC[:], func=AF.Silu,
                        scale=thr_sb[:, 6:7]), inc="SC")
                    p.wait("vector", "SC", sc_h[hoc])
                    p.op("vector", lambda e, hoc=hoc: e.scalar_tensor_tensor(
                        out=hfm[:, 512 * hoc:512 * (hoc + 1)], in0=fsD[:],
                        scalar=thr_sb[:, 9:10], in1=fsA[:],
                        op0=ALU.mult, op1=ALU.mult), inc="EV")
                    ev_h[hoc] = dve_drain() if hoc == 87 else p.v("EV")

              if NPH >= 7:
                with SB("db0", [128, 8 * H8P], BF16) as db0, \
                     SB("db1", [128, 8 * H8P], BF16) as db1, \
                     SB("x1r0", [128, D], F32) as x1r0, \
                     SB("x1r1", [128, D], F32) as x1r1, \
                     SB("ob0", [128, 128], F32) as ob0, \
                     SB("ob1", [128, 128], F32) as ob1:
                    for hf in range(2):
                        p.barrier_into(ENGS)
                        r0 = 512 * chunk + 256 * hf
                        ldx = {}
                        for tt, x1r in ((0, x1r0), (1, x1r1)):
                            ldx[tt] = p.op("sync", lambda e, r0=r0, tt=tt, x1r=x1r:
                                           e.dma_start(out=x1r[:],
                                                       in_=x1_scr[r0 + 128 * tt:
                                                                  r0 + 128 * (tt + 1), :]),
                                           inc="LD", k=16)
                        mm_end = {}
                        ev_end = {}
                        st_end = {}
                        for oc in range(32):
                            kd = oc // 4
                            ocl = oc % 4
                            db_ = db0 if oc % 2 == 0 else db1
                            if oc >= 2:
                                p.wait("sync", "MM", mm_end[oc - 2])
                            d_src = bass.AP(Dn_scr.tensor, (ocl * 128) * H8P,
                                            [[H8P, 128], [4 * 128 * H8P, 8], [1, H8P]])
                            ld_all = p.op("sync", lambda e, db_=db_, d_src=d_src:
                                          e.dma_start(out=db_[:], in_=d_src),
                                          inc="LD", k=16)
                            sgn = [SMAP[ih, kd] for ih in range(8)]
                            first = {1.0: None, -1.0: None}
                            last = {1.0: None, -1.0: None}
                            for ih in range(8):
                                if first[sgn[ih]] is None:
                                    first[sgn[ih]] = ih
                                last[sgn[ih]] = ih
                            if oc >= 1:
                                p.wait("tensor", "EV", ev_end[oc - 1])
                            p.wait("tensor", "LD", ld_all)
                            for ih in range(8):
                                j = int(JMAP[ih, kd])
                                ps = psA if sgn[ih] > 0 else psB
                                st_ = (first[sgn[ih]] == ih)
                                sp_ = (last[sgn[ih]] == ih)
                                for hs in range(11):
                                    c = 512 * (11 * ih + hs) + 256 * hf
                                    co = H8P * j + 128 * hs
                                    p.op("tensor", lambda e, db_=db_, co=co, ps=ps,
                                         c=c, st_=st_, sp_=sp_, hs=hs: e.matmul(
                                             ps[:, 0:256],
                                             db_[:, co:co + 128],
                                             hfm[:, c:c + 256],
                                             start=(st_ and hs == 0),
                                             stop=(sp_ and hs == 10)),
                                         inc="MM" if (ih == 7 and hs == 10) else None)
                            mm_end[oc] = p.v("MM")
                            p.wait("scalar", "MM", mm_end[oc])
                            if oc >= 1:
                                p.wait("scalar", "EV", ev_end[oc - 1])
                            scn = p.op("scalar", lambda e: e.activation(
                                out=fsB[:, 0:256], in_=psB[:, 0:256],
                                func=AF.Copy, bias=0.0, scale=-1.0), inc="SC")
                            p.wait("vector", "MM", mm_end[oc])
                            p.wait("vector", "SC", scn)
                            p.op("vector", lambda e: e.tensor_add(
                                fsA[:, 0:256], psA[:, 0:256], fsB[:, 0:256]),
                                inc="EV")
                            ev_end[oc] = dve_drain()
                            for tt, x1r, ob in ((0, x1r0, ob0), (1, x1r1, ob1)):
                                land, free_ev = p.fland[p.flandi % 2]
                                p.wait("tensor", "EV", max(free_ev, ev_end[oc]))
                                p.op("tensor", lambda e, land=land, tt=tt: e.transpose(
                                    land[:, 0:128], fsA[:, 128 * tt:128 * (tt + 1)],
                                    identf_sb[:]), inc="MM")
                                mmv = p.v("MM")
                                p.wait("vector", "MM", mmv)
                                p.wait("vector", "LD", ldx[tt])
                                key = (oc - 1, tt)
                                if key in st_end:
                                    p.wait("vector", "ST", st_end[key])
                                p.op("vector", lambda e, land=land, x1r=x1r, oc=oc, ob=ob:
                                     e.tensor_tensor(out=ob[:], in0=land[:, 0:128],
                                                     in1=x1r[:, 128 * oc:128 * (oc + 1)],
                                                     op=ALU.add), inc="EV")
                                p.fland[p.flandi % 2][1] = p.v("EV")
                                p.flandi += 1
                                evv = dve_drain()
                                p.wait("scalar", "EV", evv)
                                st_end[(oc, tt)] = p.op(
                                    "scalar", lambda e, r0=r0, tt=tt, oc=oc, ob=ob:
                                    e.dma_start(
                                        out=y_out[r0 + 128 * tt:r0 + 128 * (tt + 1),
                                                  128 * oc:128 * (oc + 1)],
                                        in_=ob[:]), inc="ST", k=16)
                            ev_end[oc] = p.v("EV")
                            mm_end[oc] = p.v("MM")

        p.wait("scalar", "ST", p.v("ST"))
        if DEBUG:
            dbgs = [("dbg_K", K_scr, [H * 128, SEQ], BF16),
                    ("dbg_V", V_scr, [H * 128, SEQ], BF16),
                    ("dbg_Q", Q_scr, [H * 128, NQ], BF16),
                    ("dbg_x1", x1_scr, [NQ, D], F32),
                    ("dbg_ln2", ln2_scr, [D, 512], BF16),
                    ("dbg_G", G_scr, [HIDP, D8], BF16),
                    ("dbg_D", Dn_scr, [8 * D8, H8P], BF16),
                    ("dbg_wT", wT_scr, [128 * 32, D], BF16)]
            for nm, scr, shape, dt in dbgs:
                dbg = nc.dram_tensor(nm, shape, dt, kind="ExternalOutput").ap()
                p.op("scalar", lambda e, dbg=dbg, scr=scr, shape=shape: e.dma_start(
                    out=dbg[:, :], in_=scr[0:shape[0], :]), inc="ST", k=16)
            dbg_thr = nc.dram_tensor("dbg_thr", [128, 12], F32,
                                     kind="ExternalOutput").ap()
            p.op("scalar", lambda e: e.dma_start(out=dbg_thr[:, :], in_=thr_sb[:]),
                 inc="ST", k=16)
            p.wait("scalar", "ST", p.v("ST"))

        with nc.Block() as blk:
            @blk.tensor
            def _(e):
                for f in p.q["tensor"]:
                    f(e)

            @blk.vector
            def _(e):
                for f in p.q["vector"]:
                    f(e)

            @blk.scalar
            def _(e):
                for f in p.q["scalar"]:
                    f(e)

            @blk.sync
            def _(e):
                for f in p.q["sync"]:
                    f(e)

            @blk.gpsimd
            def _(e):
                for f in p.q["gpsimd"]:
                    f(e)

    return nc


_NC_CACHE = None


def _get_nc():
    global _NC_CACHE
    if _NC_CACHE is None:
        _NC_CACHE = build_program()
    return _NC_CACHE


def kernel(L0_attn_x, L0_attn_w_qkv, L0_attn_b_qkv, L0_attn_w_out, L0_attn_b_out,
           L0_ffn_w_gate, L0_ffn_w_up, L0_ffn_w_down):
    x = np.ascontiguousarray(np.asarray(L0_attn_x, dtype=np.float32))
    w_qkv = np.ascontiguousarray(np.asarray(L0_attn_w_qkv, dtype=np.float32))
    b_qkv = np.asarray(L0_attn_b_qkv, dtype=np.float32)
    w_out = np.ascontiguousarray(np.asarray(L0_attn_w_out, dtype=np.float32))
    b_out = np.asarray(L0_attn_b_out, dtype=np.float32)
    wg = np.asarray(L0_ffn_w_gate, dtype=np.float32)
    wu = np.asarray(L0_ffn_w_up, dtype=np.float32)
    wd = np.asarray(L0_ffn_w_down, dtype=np.float32)

    bq_t = np.ascontiguousarray(b_qkv.reshape(96, 128).T)
    bo_t = np.ascontiguousarray(b_out.reshape(32, 128).T)
    wg_p = np.zeros((8, H8P, D8), np.float32)
    wg_p[:, :H8, :] = wg
    wg_p = wg_p.reshape(HIDP, D8)
    wu_p = np.zeros((8, H8P, D8), np.float32)
    wu_p[:, :H8, :] = wu
    wu_p = wu_p.reshape(HIDP, D8)
    wd_p = np.zeros((8, D8, H8P), np.float32)
    wd_p[:, :, :H8] = wd
    wd_p = wd_p.reshape(8 * D8, H8P)

    negk9 = (-1e9 * np.arange(SEQ, dtype=np.float64)).astype(np.float32).reshape(1, SEQ)
    pos = (np.arange(SEQ, dtype=np.float32) - (SEQ - 1)).reshape(1, SEQ)
    n32 = 32
    m = np.arange(1, n32 + 1, dtype=np.float32) * (8.0 / n32)
    slopes = (1.0 / np.power(2.0, m)).astype(np.float32)
    slopes_rep = np.ascontiguousarray(np.repeat(slopes.reshape(1, 32), 128, axis=0))

    nc = _get_nc()
    in_maps = []
    for c in range(8):
        b = c // 4
        q0 = 1024 * (c % 4)
        qbig = (1e9 * (q0 + np.arange(128).reshape(128, 1)
                       + 128 * np.arange(8).reshape(1, 8) + 0.5)).astype(np.float32)
        in_maps.append({
            "x_own": np.ascontiguousarray(x[b, q0:q0 + NQ]),
            "x_bat": x[b],
            "w_qkv": w_qkv,
            "w_out": w_out,
            "bq_t": bq_t,
            "bo_t": bo_t,
            "wg_p": wg_p,
            "wu_p": wu_p,
            "wd_p": wd_p,
            "qbig": qbig,
            "negk9": negk9,
            "pos": pos,
            "slopes": slopes_rep,
        })
    res = run_bass_kernel_spmd(nc, in_maps, core_ids=list(range(8)))
    out = np.empty((B, SEQ, D), np.float32)
    for c in range(8):
        b = c // 4
        q0 = 1024 * (c % 4)
        out[b, q0:q0 + NQ] = res.results[c]["y"]
    return out


# version 05 (final)
# speedup vs baseline: 1.0229x; 1.0229x over previous
import contextlib
import math

import numpy as np
import ml_dtypes

import concourse.bass as bass
import concourse.mybir as mybir
from concourse.bass_utils import run_bass_kernel_spmd

F32 = mybir.dt.float32
BF16 = mybir.dt.bfloat16
AF = mybir.ActivationFunctionType
ALU = mybir.AluOpType
AX = mybir.AxisListType
BF = ml_dtypes.bfloat16

B, SEQ, D = 2, 4096, 4096
H, HD = 32, 128
NQ = 1024
QKV_O = 3 * D
HID, HIDP = 10928, 11264
H8, H8P, D8 = 1366, 1408, 512
EPS = 1e-5
QSCALE = 1.0 / math.sqrt(HD)
NW_ELEM = 8 * H8 * D8

ENGS = ("tensor", "vector", "scalar", "sync", "gpsimd")
SEMS = ("LD", "ST", "MM", "EV", "SC", "GP")

_TRIPLES = [(1, 2, 3), (1, 4, 5), (1, 7, 6), (2, 4, 6), (2, 5, 7), (3, 4, 7), (3, 6, 5)]


def _oct_maps():
    S = np.zeros((8, 8, 8), np.float32)
    S[0, 0, 0] = 1.0
    for i in range(1, 8):
        S[0, i, i] = 1.0
        S[i, 0, i] = 1.0
        S[i, i, 0] = -1.0
    for a, b, c in _TRIPLES:
        for x, y, z in ((a, b, c), (b, c, a), (c, a, b)):
            S[x, y, z] = 1.0
            S[y, x, z] = -1.0
    jm = np.zeros((8, 8), np.int32)
    sm = np.zeros((8, 8), np.float32)
    for i in range(8):
        for k in range(8):
            js = [j for j in range(8) if S[j, i, k] != 0]
            jm[i, k] = js[0]
            sm[i, k] = S[js[0], i, k]
    return jm, sm


JMAP, SMAP = _oct_maps()

NPH = 99
DEBUG = False
STT_MIN = True
SC_EVAC = True
ASTAGE = 99


class P:

    def __init__(self, nc, es):
        self.nc = nc
        self.q = {k: [] for k in ENGS}
        self.sem = {}
        self.cnt = {}
        for name in SEMS:
            self.sem[name] = es.enter_context(nc.semaphore(name))
            self.cnt[name] = 0
        self.land = None
        self.landi = 0
        self.flandi = 0

    def v(self, name):
        return self.cnt[name]

    def snap(self):
        return dict(self.cnt)

    def op(self, eng, fn, inc=None, k=1):
        if inc is None:
            self.q[eng].append(fn)
            return None
        sem = self.sem[inc]

        def g(e, fn=fn, sem=sem, k=k):
            fn(e).then_inc(sem, k)

        self.q[eng].append(g)
        self.cnt[inc] += k
        return self.cnt[inc]

    def wait(self, eng, name, val):
        if val <= 0:
            return
        sem = self.sem[name]
        self.q[eng].append(lambda e, sem=sem, val=val: e.wait_ge(sem, val))

    def wait_all(self, eng, snap):
        for name in SEMS:
            if snap[name] > 0:
                self.wait(eng, name, snap[name])

    def barrier_into(self, engs):
        s = self.snap()
        for eng in engs:
            self.wait_all(eng, s)


def build_program():
    nc = bass.Bass("TRN2", target_bir_lowering=False, debug=False)

    x_own = nc.dram_tensor("x_own", [NQ, D], F32, kind="ExternalInput").ap()
    x_bat = nc.dram_tensor("x_bat", [SEQ, D], F32, kind="ExternalInput").ap()
    w_qkv = nc.dram_tensor("w_qkv", [QKV_O, D], F32, kind="ExternalInput").ap()
    w_out = nc.dram_tensor("w_out", [D, D], F32, kind="ExternalInput").ap()
    bq_t = nc.dram_tensor("bq_t", [128, 96], F32, kind="ExternalInput").ap()
    bo_t = nc.dram_tensor("bo_t", [128, 32], F32, kind="ExternalInput").ap()
    wg_p = nc.dram_tensor("wg_p", [HIDP, D8], F32, kind="ExternalInput").ap()
    wu_p = nc.dram_tensor("wu_p", [HIDP, D8], F32, kind="ExternalInput").ap()
    wd_p = nc.dram_tensor("wd_p", [8 * D8, H8P], F32, kind="ExternalInput").ap()
    qbig_in = nc.dram_tensor("qbig", [128, 8], F32, kind="ExternalInput").ap()
    negk9_in = nc.dram_tensor("negk9", [1, SEQ], F32, kind="ExternalInput").ap()
    pos_in = nc.dram_tensor("pos", [1, SEQ], F32, kind="ExternalInput").ap()
    slopes_in = nc.dram_tensor("slopes", [128, H], F32, kind="ExternalInput").ap()

    y_out = nc.dram_tensor("y", [NQ, D], F32, kind="ExternalOutput").ap()

    wT_scr = nc.dram_tensor("wT_scr", [128 * 128, D], BF16, kind="Internal").ap()
    K_scr = nc.dram_tensor("K_scr", [H * 128, SEQ], BF16, kind="Internal").ap()
    V_scr = nc.dram_tensor("V_scr", [H * 128, SEQ], BF16, kind="Internal").ap()
    Q_scr = nc.dram_tensor("Q_scr", [H * 128, NQ], BF16, kind="Internal").ap()
    G_scr = nc.dram_tensor("G_scr", [HIDP, D8], BF16, kind="Internal").ap()
    U_scr = nc.dram_tensor("U_scr", [HIDP, D8], BF16, kind="Internal").ap()
    Dn_scr = nc.dram_tensor("Dn_scr", [8 * D8, H8P], BF16, kind="Internal").ap()
    x1_scr = nc.dram_tensor("x1_scr", [NQ, D], F32, kind="Internal").ap()
    ln2_scr = nc.dram_tensor("ln2_scr", [D, 512], BF16, kind="Internal").ap()

    ident_in = nc.inline_tensor(np.eye(128, dtype=BF), "identb").ap()
    identf_in = nc.inline_tensor(np.eye(128, dtype=np.float32), "identf").ap()
    onesf_in = nc.inline_tensor(np.ones((128, 1), dtype=np.float32), "onesf").ap()
    onesr_in = nc.inline_tensor(np.ones((1, 128), dtype=np.float32), "onesr").ap()

    es = contextlib.ExitStack()
    with es:
        p = P(nc, es)
        _sbctr = iter(range(100000))
        SB = lambda name, shape, dt: nc.sbuf_tensor(
            f"{name}_{next(_sbctr)}", shape, dt)
        stat = lambda name, shape, dt: es.enter_context(SB(name, shape, dt))

        ident_sb = stat("ident_sb", [128, 128], BF16)
        identf_sb = stat("identf_sb", [128, 128], F32)
        onesf_sb = stat("onesf_sb", [128, 1], F32)
        onesr_sb = stat("onesr_sb", [1, 128], F32)
        bq_sb = stat("bq_sb", [128, 96], F32)
        bo_sb = stat("bo_sb", [128, 32], F32)
        qbig_sb = stat("qbig_sb", [128, 8], F32)
        pos_sb = stat("pos_sb", [1, SEQ], F32)
        slopes_sb = stat("slopes_sb", [128, H], F32)
        thr_sb = stat("thr_sb", [128, 12], F32)
        lnst = stat("lnst", [128, 16], F32)
        sm_max = stat("sm_max", [128, 8], F32)
        sm_sum = stat("sm_sum", [128, 8], F32)
        sm_ln = stat("sm_ln", [128, 8], F32)
        sm_rec = stat("sm_rec", [128, 8], F32)
        sq_sb = stat("sq_sb", [128, D], F32)

        fsA = stat("fsA", [128, 512], F32)
        fsB = stat("fsB", [128, 512], F32)
        fsC = stat("fsC", [128, 512], F32)
        fsD = stat("fsD", [128, 512], F32)
        drn = stat("drn", [128, 2], F32)

        def dve_drain():
            g1 = p.op("vector", lambda e: e.tensor_copy(drn[:, 0:1], drn[:, 1:2]),
                      inc="EV")
            p.wait("vector", "EV", g1)
            g2 = p.op("vector", lambda e: e.tensor_copy(drn[:, 0:1], drn[:, 1:2]),
                      inc="EV")
            return g2

        def gp_drain():
            g1 = p.op("gpsimd", lambda e: e.tensor_copy(drn[:, 0:1], drn[:, 1:2]),
                      inc="GP")
            p.wait("gpsimd", "GP", g1)
            g2 = p.op("gpsimd", lambda e: e.tensor_copy(drn[:, 0:1], drn[:, 1:2]),
                      inc="GP")
            return g2

        psA = es.enter_context(nc.psum_tensor("psA", [128, 512], F32))
        psB = es.enter_context(nc.psum_tensor("psB", [128, 512], F32))
        psC = es.enter_context(nc.psum_tensor("psC", [128, 512], F32))
        psD = es.enter_context(nc.psum_tensor("psD", [128, 512], F32))
        psE = es.enter_context(nc.psum_tensor("psE", [128, 512], F32))
        psF = es.enter_context(nc.psum_tensor("psF", [128, 512], F32))
        psT = es.enter_context(nc.psum_tensor("psT", [128, 512], BF16))
        psU = es.enter_context(nc.psum_tensor("psU", [128, 512], BF16))

        p.land = [[psT, 0], [psU, 0]]
        p.fland = [[psE, 0], [psF, 0]]

        def tr_batch(srcs, dsts, f32=False, extra_src_wait=None):
            idt = identf_sb if f32 else ident_sb
            lands = p.fland if f32 else p.land
            gsz = 2 if f32 else 4
            n = len(srcs)
            gi = 0
            while gi * gsz < n:
                lo = gi * gsz
                hi = min(lo + gsz, n)
                land, free_ev = lands[gi % 2] if False else lands[(p.flandi if f32 else p.landi) % 2]
                p.wait("tensor", "EV", free_ev)
                if extra_src_wait is not None:
                    w = extra_src_wait(lo, hi)
                    if w:
                        p.wait("tensor", w[0], w[1])
                for j, si in enumerate(range(lo, hi)):
                    sbuf, scol = srcs[si]
                    p.op("tensor",
                         lambda e, land=land, j=j, sbuf=sbuf, scol=scol, idt=idt:
                         e.transpose(land[:, 128 * j:128 * (j + 1)],
                                     sbuf[:, scol:scol + 128], idt[:]),
                         inc="MM" if si == hi - 1 else None)
                mmv = p.v("MM")
                p.wait("vector", "MM", mmv)
                for j, si in enumerate(range(lo, hi)):
                    dbuf_, dcol = dsts[si]
                    p.op("vector",
                         lambda e, land=land, j=j, dbuf_=dbuf_, dcol=dcol:
                         e.tensor_copy(dbuf_[:, dcol:dcol + 128],
                                       land[:, 128 * j:128 * (j + 1)]),
                         inc="EV" if si == hi - 1 else None)
                if f32:
                    p.fland[p.flandi % 2][1] = p.v("EV")
                    p.flandi += 1
                else:
                    p.land[p.landi % 2][1] = p.v("EV")
                    p.landi += 1
                gi += 1

        p.op("vector", lambda e: e.memset(drn[:], 0.0), inc="EV")
        for out_t, in_t in (
            (ident_sb, ident_in), (identf_sb, identf_in), (onesf_sb, onesf_in),
            (onesr_sb, onesr_in), (bq_sb, bq_t), (bo_sb, bo_t),
            (qbig_sb, qbig_in), (pos_sb, pos_in), (slopes_sb, slopes_in),
        ):
            p.op("sync", lambda e, o=out_t, i=in_t: e.dma_start(out=o[:], in_=i[:, :]),
                 inc="LD", k=16)

        with SB("wsA", [128, D], BF16) as wsA, SB("wsB", [128, D], BF16) as wsB, \
             SB("wtA", [128, D], BF16) as wtA, SB("wtB", [128, D], BF16) as wtB, \
             SB("wfA", [128, D], F32) as wfA, SB("wfB", [128, D], F32) as wfB:
            p.barrier_into(("sync", "scalar"))
            mm_end = {}
            ev_end = {}
            st_end = {}
            ld_end = {}
            cast_sc = {}
            for oc in range(128):
                src = (w_qkv[128 * oc:128 * (oc + 1), :] if oc < 96 else
                       w_out[128 * (oc - 96):128 * (oc - 95), :])
                buf = wsA if oc % 2 == 0 else wsB
                wf = wfA if oc % 2 == 0 else wfB
                wt = wtA if oc % 2 == 0 else wtB
                if oc >= 2:
                    p.wait("sync", "SC", cast_sc[oc - 2])
                ld_end[oc] = p.op("sync",
                                  lambda e, wf=wf, src=src: e.dma_start(out=wf[:], in_=src),
                                  inc="LD", k=16)
                p.wait("scalar", "LD", ld_end[oc])
                if oc >= 2:
                    p.wait("scalar", "MM", mm_end[oc - 2])
                cast_sc[oc] = p.op("scalar", lambda e, wf=wf, buf=buf: e.activation(
                    out=buf[:], in_=wf[:], func=AF.Copy, bias=0.0, scale=1.0),
                    inc="SC")
                p.wait("tensor", "SC", cast_sc[oc])
                if oc >= 2:
                    p.wait("vector", "ST", st_end[oc - 2])
                tr_batch([(buf, 128 * ds) for ds in range(32)],
                         [(wt, 128 * ds) for ds in range(32)])
                mm_end[oc] = p.v("MM")
                ev_end[oc] = dve_drain()
                p.wait("scalar", "EV", ev_end[oc])
                st_end[oc] = p.op("scalar",
                                  lambda e, oc=oc, wt=wt: e.dma_start(
                                      out=wT_scr[128 * oc:128 * (oc + 1), :], in_=wt[:]),
                                  inc="ST", k=16)

            def dequant(tag, w_in, scr, n_rt, ncols, n_tr, col):
                p.barrier_into(ENGS)
                with SB("acc" + tag, [128, n_rt], F32) as acc, \
                     SB("qa" + tag, [128, ncols], BF16) as qA, \
                     SB("qb" + tag, [128, ncols], BF16) as qB, \
                     SB("sa" + tag, [128, ncols], BF16) as sA, \
                     SB("sb" + tag, [128, ncols], BF16) as sB, \
                     SB("f0" + tag, [128, ncols], F32) as f0, \
                     SB("f1" + tag, [128, ncols], F32) as f1, \
                     SB("t0" + tag, [128, ncols], F32) as t0, \
                     SB("t1" + tag, [128, ncols], F32) as t1, \
                     SB("u0" + tag, [128, ncols], F32) as u0, \
                     SB("u1" + tag, [128, ncols], F32) as u1:
                    sc_end = {}
                    for r in range(n_rt):
                        buf = f0 if r % 2 == 0 else f1
                        if r >= 2:
                            p.wait("sync", "SC", sc_end[r - 2])
                        ldv = p.op("sync", lambda e, buf=buf, r=r: e.dma_start(
                            out=buf[:], in_=w_in[128 * r:128 * (r + 1), :]),
                            inc="LD", k=16)
                        p.wait("scalar", "LD", ldv)
                        sc_end[r] = p.op("scalar", lambda e, buf=buf, r=r: e.activation(
                            out=sq_sb[:, 0:ncols], in_=buf[:], func=AF.Abs,
                            accum_out=acc[:, r:r + 1]), inc="SC")
                    p.wait("vector", "SC", sc_end[n_rt - 1])
                    ev1 = p.op("vector", lambda e: e.reduce_sum(
                        fsA[:, 0:1], acc[:, 0:n_rt], axis=AX.X), inc="EV")
                    p.wait("tensor", "EV", ev1)
                    p.wait("tensor", "EV", p.fland[0][1])
                    mm1 = p.op("tensor", lambda e: e.matmul(
                        psE[0:1, 0:1], fsA[:, 0:1], onesf_sb[:],
                        start=True, stop=True), inc="MM")
                    p.wait("vector", "MM", mm1)
                    p.op("vector", lambda e: e.tensor_scalar_mul(
                        fsB[0:1, 0:1], psE[0:1, 0:1], 0.5 / NW_ELEM))
                    ev2 = p.op("vector", lambda e: e.tensor_scalar_mul(
                        fsB[0:1, 1:2], psE[0:1, 0:1], 1.0 / NW_ELEM), inc="EV")
                    p.wait("tensor", "EV", ev2)
                    mm2 = p.op("tensor", lambda e: e.matmul(
                        psE[:, 0:2], onesr_sb[:], fsB[0:1, 0:2],
                        start=True, stop=True), inc="MM")
                    p.wait("vector", "MM", mm2)
                    p.op("vector", lambda e, col=col: e.tensor_copy(
                        thr_sb[:, col:col + 1], psE[:, 0:1]))
                    p.op("vector", lambda e, col=col: e.tensor_scalar_mul(
                        thr_sb[:, col + 3:col + 4], psE[:, 0:1], -1.0))
                    ev3 = p.op("vector", lambda e, col=col: e.tensor_copy(
                        thr_sb[:, col + 6:col + 7], psE[:, 1:2]), inc="EV")
                    p.fland[0][1] = ev3

                    tern_gp = {}
                    st_end2 = {}
                    mm_end2 = {}
                    cmp_ev = {}
                    for r in range(n_rt):
                        buf = f0 if r % 2 == 0 else f1
                        qt = qA if r % 2 == 0 else qB
                        stg = sA if r % 2 == 0 else sB
                        ge_ = t0 if r % 2 == 0 else t1
                        le_ = u0 if r % 2 == 0 else u1
                        if r >= 2:
                            p.wait("sync", "GP", tern_gp[r - 2])
                        else:
                            p.wait("sync", "SC", sc_end[n_rt - 1])
                        ldv = p.op("sync", lambda e, buf=buf, r=r: e.dma_start(
                            out=buf[:], in_=w_in[128 * r:128 * (r + 1), :]),
                            inc="LD", k=16)
                        p.wait("vector", "LD", ldv)
                        p.wait("vector", "EV", ev3)
                        if r >= 2:
                            p.wait("vector", "GP", tern_gp[r - 2])
                        p.op("vector", lambda e, buf=buf, col=col, ge_=ge_:
                             e.tensor_scalar(
                                 out=ge_[:], in0=buf[:],
                                 scalar1=thr_sb[:, col:col + 1],
                                 scalar2=None, op0=ALU.is_ge))
                        cmp_ev[r] = p.op("vector", lambda e, buf=buf, col=col,
                                         le_=le_: e.tensor_scalar(
                            out=le_[:], in0=buf[:],
                            scalar1=thr_sb[:, col + 3:col + 4],
                            scalar2=None, op0=ALU.is_le), inc="EV")
                        p.wait("gpsimd", "EV", cmp_ev[r])
                        if r >= 2:
                            p.wait("gpsimd", "MM", mm_end2[r - 2])
                        tern_gp[r] = p.op("gpsimd", lambda e, qt=qt, ge_=ge_, le_=le_:
                                          e.tensor_sub(qt[:], ge_[:], le_[:]),
                                          inc="GP")
                        tern_gp[r] = gp_drain()
                        p.wait("tensor", "GP", tern_gp[r])
                        if r >= 2:
                            p.wait("vector", "ST", st_end2[r - 2])
                        tr_batch([(qt, 128 * t) for t in range(n_tr)],
                                 [(stg, 128 * t) for t in range(n_tr)])
                        mm_end2[r] = p.v("MM")
                        p.wait("scalar", "EV", dve_drain())
                        st_end2[r] = p.op("scalar", lambda e, r=r, stg=stg: e.dma_start(
                            out=scr[128 * r:128 * (r + 1), :], in_=stg[:]),
                            inc="ST", k=16)

            if NPH >= 2:
                dequant("g", wg_p, G_scr, HIDP // 128, D8, 4, 0)
                dequant("u", wu_p, U_scr, HIDP // 128, D8, 4, 1)
                dequant("d", wd_p, Dn_scr, 8 * D8 // 128, H8P, 11, 2)

            p.op("vector", lambda e: e.tensor_mul(
                thr_sb[:, 9:10], thr_sb[:, 7:8], thr_sb[:, 8:9]), inc="EV")

        def ln_tiles(tiles, outs, ld_gate):
            p.wait("scalar", "LD", ld_gate)
            for ti, (xbuf, sl) in enumerate(tiles):
                p.op("scalar", lambda e, xbuf=xbuf, sl=sl: e.activation(
                    out=sq_sb[:], in_=xbuf[:], func=AF.Square,
                    accum_out=lnst[:, 8 * sl + 7:8 * sl + 8]),
                    inc="SC" if ti == len(tiles) - 1 else None)
            sc1 = p.v("SC")
            p.wait("vector", "LD", ld_gate)
            for xbuf, sl in tiles:
                p.op("vector", lambda e, xbuf=xbuf, sl=sl: e.reduce_sum(
                    lnst[:, 8 * sl:8 * sl + 1], xbuf[:], axis=AX.X))
            for xbuf, sl in tiles:
                p.op("vector", lambda e, sl=sl: e.tensor_scalar_mul(
                    lnst[:, 8 * sl + 1:8 * sl + 2], lnst[:, 8 * sl:8 * sl + 1],
                    -1.0 / D))
            for xbuf, sl in tiles:
                p.op("vector", lambda e, sl=sl: e.tensor_mul(
                    lnst[:, 8 * sl + 2:8 * sl + 3], lnst[:, 8 * sl + 1:8 * sl + 2],
                    lnst[:, 8 * sl + 1:8 * sl + 2]))
            p.wait("vector", "SC", sc1)
            for xbuf, sl in tiles:
                p.op("vector", lambda e, sl=sl: e.tensor_scalar(
                    out=lnst[:, 8 * sl + 3:8 * sl + 4],
                    in0=lnst[:, 8 * sl + 7:8 * sl + 8],
                    scalar1=1.0 / D, scalar2=EPS, op0=ALU.mult, op1=ALU.add))
            for ti, (xbuf, sl) in enumerate(tiles):
                p.op("vector", lambda e, sl=sl: e.tensor_sub(
                    lnst[:, 8 * sl + 4:8 * sl + 5], lnst[:, 8 * sl + 3:8 * sl + 4],
                    lnst[:, 8 * sl + 2:8 * sl + 3]),
                    inc="EV" if ti == len(tiles) - 1 else None)
            ev1 = p.v("EV")
            p.wait("scalar", "EV", ev1)
            for xbuf, sl in tiles:
                p.op("scalar", lambda e, sl=sl: e.activation(
                    out=lnst[:, 8 * sl + 5:8 * sl + 6],
                    in_=lnst[:, 8 * sl + 4:8 * sl + 5], func=AF.Ln))
            for ti, (xbuf, sl) in enumerate(tiles):
                p.op("scalar", lambda e, sl=sl: e.activation(
                    out=lnst[:, 8 * sl + 6:8 * sl + 7],
                    in_=lnst[:, 8 * sl + 5:8 * sl + 6], func=AF.Exp, scale=-0.5),
                    inc="SC" if ti == len(tiles) - 1 else None)
            sc2 = p.v("SC")
            p.wait("vector", "SC", sc2)
            for ti, ((xbuf, sl), ob) in enumerate(zip(tiles, outs)):
                p.op("vector", lambda e, xbuf=xbuf, sl=sl, ob=ob: e.tensor_scalar(
                    out=ob[:], in0=xbuf[:],
                    scalar1=lnst[:, 8 * sl + 1:8 * sl + 2],
                    scalar2=lnst[:, 8 * sl + 6:8 * sl + 7],
                    op0=ALU.add, op1=ALU.mult),
                    inc="EV" if ti == len(tiles) - 1 else None)
            return p.v("EV")

        with SB("fmA3", [128, 32 * 1024], BF16) as fmA, \
             SB("evb0", [128, 512], BF16) as evb0, \
             SB("evb1", [128, 512], BF16) as evb1, \
             SB("evb2", [128, 512], BF16) as evb2, \
             SB("evb3", [128, 512], BF16) as evb3, \
             SB("xa3", [128, D], F32) as xa, \
             SB("xb3", [128, D], F32) as xb, \
             SB("lbA", [128, D], BF16) as lbA, \
             SB("lbB", [128, D], BF16) as lbB, \
             SB("wsA3", [128, D], BF16) as wsA, \
             SB("wsB3", [128, D], BF16) as wsB, \
             SB("vsA", [128, 512], BF16) as vsA, \
             SB("vsB", [128, 512], BF16) as vsB:

            def proj_pass(x_src, tb, oc_list):
                p.barrier_into(ENGS)
                norm_ev = {}
                mm_tr = {}
                for pr in range(4):
                    if pr >= 1:
                        p.wait("sync", "EV", norm_ev[pr - 1])
                    for half in range(2):
                        tt = 2 * pr + half
                        buf = xa if half == 0 else xb
                        ldv = p.op("sync", lambda e, buf=buf, tt=tt: e.dma_start(
                            out=buf[:], in_=x_src[128 * tt:128 * (tt + 1), :]),
                            inc="LD", k=16)
                    if pr >= 1:
                        p.wait("vector", "MM", mm_tr[pr - 1])
                    norm_ev[pr] = ln_tiles([(xa, 0), (xb, 1)], [lbA, lbB], ldv)
                    for half in range(2):
                        tt = 2 * pr + half
                        lb = lbA if half == 0 else lbB
                        p.wait("tensor", "EV", norm_ev[pr])
                        tr_batch([(lb, 128 * ds) for ds in range(32)],
                                 [(fmA, 1024 * ds + 128 * tt) for ds in range(32)])
                    mm_tr[pr] = p.v("MM")
                ev_ln = dve_drain()

                mm_end = {}
                ev_end = {}
                st_end = {}
                p.wait("tensor", "EV", ev_ln)
                for idx, oc in enumerate(oc_list):
                    is_v = 64 <= oc < 96
                    buf = wsA if idx % 2 == 0 else wsB
                    ev0_ = (evb0, evb2)[idx % 2]
                    ev1_ = (evb1, evb3)[idx % 2]
                    ps0 = psA if idx % 2 == 0 else psC
                    ps1 = psB if idx % 2 == 0 else psD
                    if idx >= 2:
                        p.wait("sync", "MM", mm_end[idx - 2])
                    ldv = p.op("sync", lambda e, buf=buf, oc=oc: e.dma_start(
                        out=buf[:], in_=wT_scr[128 * oc:128 * (oc + 1), :]),
                        inc="LD", k=16)
                    p.wait("tensor", "LD", ldv)
                    if idx >= 2:
                        p.wait("tensor", "EV", ev_end[idx - 2])
                    for ds in range(32):
                        for strip, ps in ((0, ps0), (1, ps1)):
                            p.op("tensor", lambda e, buf=buf, ds=ds, strip=strip, ps=ps:
                                 e.matmul(ps[:],
                                          buf[:, 128 * ds:128 * (ds + 1)],
                                          fmA[:, 1024 * ds + 512 * strip:
                                              1024 * ds + 512 * strip + 512],
                                          start=(ds == 0), stop=(ds == 31)),
                                 inc="MM" if (ds == 31) else None)
                    mm_end[idx] = p.v("MM")
                    p.wait("vector", "MM", mm_end[idx])
                    if idx >= 2:
                        p.wait("vector", "ST", st_end[idx - 2])
                    for strip, ps, eb in ((0, ps0, ev0_), (1, ps1, ev1_)):
                        if oc < 32:
                            p.op("vector", lambda e, ps=ps, eb=eb, oc=oc: e.tensor_scalar(
                                out=eb[:], in0=ps[:], scalar1=bq_sb[:, oc:oc + 1],
                                scalar2=QSCALE, op0=ALU.add, op1=ALU.mult),
                                inc="EV")
                        else:
                            p.op("vector", lambda e, ps=ps, eb=eb, oc=oc: e.tensor_scalar(
                                out=eb[:], in0=ps[:], scalar1=bq_sb[:, oc:oc + 1],
                                scalar2=None, op0=ALU.add), inc="EV")
                    ev_end[idx] = p.v("EV")
                    if not is_v:
                        evd = dve_drain()
                        for strip, eb in ((0, ev0_), (1, ev1_)):
                            if oc < 32:
                                dst = Q_scr[128 * oc:128 * (oc + 1),
                                            512 * strip:512 * (strip + 1)]
                            else:
                                h = oc - 32
                                c0 = 1024 * tb + 512 * strip
                                dst = K_scr[128 * h:128 * (h + 1), c0:c0 + 512]
                            p.wait("scalar", "EV", evd)
                            p.op("scalar", lambda e, dst=dst, eb=eb: e.dma_start(
                                out=dst, in_=eb[:]), inc="ST", k=16)
                    else:
                        h = oc - 64
                        evd0 = dve_drain()
                        for strip, eb, vs in ((0, ev0_, vsA), (1, ev1_, vsB)):
                            if idx >= 1 and oc > 64:
                                p.wait("vector", "ST", st_end[idx - 1])
                            p.wait("tensor", "EV", evd0)
                            tr_batch([(eb, 128 * t) for t in range(4)],
                                     [(vs, 128 * t) for t in range(4)])
                            evv = dve_drain()
                            c0 = (8 * tb + 4 * strip) * 128
                            p.wait("scalar", "EV", evv)
                            p.op("scalar", lambda e, h=h, c0=c0, vs=vs: e.dma_start(
                                out=V_scr[128 * h:128 * (h + 1), c0:c0 + 512],
                                in_=vs[:]), inc="ST", k=16)
                    st_end[idx] = p.v("ST")
                    ev_end[idx] = p.v("EV")
                    mm_end[idx] = p.v("MM")

            if NPH >= 3:
                for tb in range(4):
                    proj_pass(x_bat[1024 * tb:1024 * (tb + 1), :], tb,
                              list(range(32, 96)))
                proj_pass(x_own[:, :], 0, list(range(0, 32)))

        for chunk in range(2 if NPH >= 4 else 0):
            g_fmB = SB("fmB", [128, 32 * 512], BF16)
            fmB = g_fmB.__enter__()
            g_ph4 = [SB("fmA4", [128, 4 * SEQ + 16 * 512], BF16),
                     SB("alibi_s", [128, SEQ], F32),
                     SB("posbc_s", [128, SEQ], F32),
                     SB("negk9_s", [128, SEQ], F32),
                     SB("Ksb", [128, SEQ], BF16),
                     SB("Vsb", [128, SEQ], BF16),
                     SB("Qsb", [128, 512], BF16),
                     SB("xa4", [128, D], F32)]
            (fmA4, alibi_sb, posbc_sb, negk9_sb, Ksb, Vsb, Qsb, xa) = [
                g.__enter__() for g in g_ph4]
            if True:
                p.barrier_into(("sync",))
                src = bass.AP(negk9_in.tensor, 0, [[0, 128], [1, SEQ]])
                p.op("sync", lambda e, src=src: e.dma_start(out=negk9_sb[:], in_=src),
                     inc="LD", k=16)
                src2 = bass.AP(pos_in.tensor, 0, [[0, 128], [1, SEQ]])
                p.op("sync", lambda e, src2=src2: e.dma_start(out=posbc_sb[:], in_=src2),
                     inc="LD", k=16)
                p.barrier_into(ENGS)

                head_mm = {}
                head_gp = {}
                head_ev = {}
                head_sc = {}
                head_evread = {}
                head_scmm = {}
                for h in range(H):
                    if h >= 1:
                        p.wait("sync", "MM", head_scmm[h - 1])
                    ldk = p.op("sync", lambda e, h=h: e.dma_start(
                        out=Ksb[:], in_=K_scr[128 * h:128 * (h + 1), :]),
                        inc="LD", k=16)
                    if h >= 1:
                        p.wait("sync", "MM", head_mm[h - 1])
                        p.wait("sync", "EV", head_ev[h - 1])
                    ldv = p.op("sync", lambda e, h=h: e.dma_start(
                        out=Vsb[:], in_=V_scr[128 * h:128 * (h + 1), :]),
                        inc="LD", k=16)
                    ldq = p.op("sync", lambda e, h=h, chunk=chunk: e.dma_start(
                        out=Qsb[:], in_=Q_scr[128 * h:128 * (h + 1),
                                              512 * chunk:512 * (chunk + 1)]),
                        inc="LD", k=16)
                    if ASTAGE >= 2:
                        if h >= 1:
                            p.wait("gpsimd", "EV", head_evread[h - 1])
                        p.op("gpsimd", lambda e, h=h: e.tensor_scalar_mul(
                            alibi_sb[:], posbc_sb[:], slopes_sb[:, h:h + 1]),
                            inc="GP")
                        alibi_gp = gp_drain()

                    stt_end = {}
                    gp_end = {}
                    sc_norm = {}
                    p.wait("tensor", "LD", ldq)
                    for qt in range(4 if ASTAGE >= 3 else 0):
                        qtg = 4 * chunk + qt
                        for kc in range(8):
                            ki = qt * 8 + kc
                            ps = (psA, psB, psC, psD)[ki % 4]
                            if SC_EVAC and ki >= 4:
                                p.wait("tensor", "EV", stt_end[ki - 4])
                            p.op("tensor", lambda e, qt=qt, kc=kc, ps=ps: e.matmul(
                                ps[:], Qsb[:, 128 * qt:128 * (qt + 1)],
                                Ksb[:, 512 * kc:512 * (kc + 1)],
                                start=True, stop=True), inc="MM")
                            if not SC_EVAC:
                                stt_end[ki] = p.v("EV")
                                continue
                            p.wait("vector", "MM", p.v("MM"))
                            if kc == 0:
                                if qt >= 1:
                                    p.wait("vector", "SC", sc_norm[qt - 1] - 3)
                                elif h >= 1:
                                    p.wait("vector", "SC", head_sc[h - 1] - 3)
                                if qt == 0:
                                    p.wait("vector", "GP", alibi_gp)
                            fs_ = fsA if kc % 2 == 0 else fsB
                            p.op("vector",
                                 lambda e, fs_=fs_, kc=kc, ps=ps, qtg=qtg:
                                 e.scalar_tensor_tensor(
                                     out=fs_[:],
                                     in0=negk9_sb[:, 512 * kc:512 * (kc + 1)],
                                     scalar=qbig_sb[:, qtg:qtg + 1],
                                     in1=ps[:], op0=ALU.add, op1=ALU.min))
                            stt_end[ki] = p.op(
                                "vector",
                                lambda e, fs_=fs_, kc=kc: e.tensor_tensor(
                                    out=sq_sb[:, 512 * kc:512 * (kc + 1)],
                                    in0=fs_[:],
                                    in1=alibi_sb[:, 512 * kc:512 * (kc + 1)],
                                    op=ALU.add),
                                inc="EV")
                        if ASTAGE < 4:
                            gp_end[qt] = p.v("GP")
                            sc_norm[qt] = p.v("SC") + 3
                            continue
                        p.op("vector", lambda e: e.tensor_copy(drn[:, 0:1],
                                                               drn[:, 1:2]))
                        evm = p.op("vector", lambda e, qt=qt: e.reduce_max(
                            sm_max[:, qt:qt + 1], sq_sb[:], axis=AX.X, negate=True),
                            inc="EV")
                        gp_end[qt] = p.v("GP")
                        p.wait("scalar", "EV", evm)
                        s1 = p.op("scalar", lambda e, qt=qt: e.activation(
                            out=xa[:], in_=sq_sb[:], func=AF.Exp,
                            bias=sm_max[:, qt:qt + 1],
                            accum_out=sm_sum[:, qt:qt + 1]), inc="SC")
                        p.wait("scalar", "SC", s1)
                        s2 = p.op("scalar", lambda e, qt=qt: e.activation(
                            out=sm_ln[:, qt:qt + 1], in_=sm_sum[:, qt:qt + 1],
                            func=AF.Ln), inc="SC")
                        p.wait("scalar", "SC", s2)
                        s3 = p.op("scalar", lambda e, qt=qt: e.activation(
                            out=sm_rec[:, qt:qt + 1], in_=sm_ln[:, qt:qt + 1],
                            func=AF.Exp, scale=-1.0), inc="SC")
                        p.wait("scalar", "SC", s3)
                        sc_norm[qt] = p.op("scalar", lambda e, qt=qt: e.activation(
                            out=fmA4[:, SEQ * qt:SEQ * (qt + 1)], in_=xa[:],
                            func=AF.Copy, bias=0.0,
                            scale=sm_rec[:, qt:qt + 1]), inc="SC")

                    head_scmm[h] = p.v("MM")
                    if ASTAGE >= 4:
                        p.wait("tensor", "SC", sc_norm[3])
                    p.wait("tensor", "LD", ldv)
                    av_mm = {}
                    for khalf in range(2 if ASTAGE >= 5 else 0):
                        pt_ev = {}
                        for kt in range(16):
                            ktg = 16 * khalf + kt
                            if khalf == 1:
                                p.wait("vector", "MM", av_mm[kt])
                            tr_batch(
                                [(fmA4, SEQ * qt + 128 * ktg) for qt in range(4)],
                                [(fmA4, 4 * SEQ + 512 * kt + 128 * qt)
                                 for qt in range(4)])
                            pt_ev[kt] = p.v("EV")
                        kh_drain = dve_drain()
                        for kt in range(16):
                            pt_ev[kt] = kh_drain
                        for kt in range(16 if ASTAGE >= 6 else 0):
                            ktg = 16 * khalf + kt
                            if khalf == 0 and kt == 0 and h >= 1:
                                p.wait("tensor", "EV", head_ev[h - 1])
                            p.wait("tensor", "EV", pt_ev[kt])
                            avv = p.op("tensor", lambda e, ktg=ktg, kt=kt, khalf=khalf:
                                       e.matmul(
                                           psE[:], Vsb[:, 128 * ktg:128 * (ktg + 1)],
                                           fmA4[:, 4 * SEQ + 512 * kt:
                                                4 * SEQ + 512 * (kt + 1)],
                                           start=(khalf == 0 and kt == 0),
                                           stop=(khalf == 1 and kt == 15)), inc="MM")
                            if khalf == 0:
                                av_mm[kt] = avv
                    head_mm[h] = p.v("MM")
                    if ASTAGE >= 6:
                        p.wait("vector", "MM", head_mm[h])
                        p.op("vector", lambda e, h=h: e.tensor_copy(
                            fmB[:, 512 * h:512 * (h + 1)], psE[:]), inc="EV")
                        head_ev[h] = dve_drain()
                        p.fland[0][1] = head_ev[h]
                    else:
                        head_ev[h] = p.v("EV")
                    head_gp[h] = gp_end.get(3, p.v("GP"))
                    head_sc[h] = sc_norm.get(3, p.v("SC") + 3)
                    head_evread[h] = stt_end.get(31, p.v("EV"))

                for g in reversed(g_ph4):
                    g.__exit__(None, None, None)

                if NPH < 5:
                    g_fmB.__exit__(None, None, None)
                    continue
                with SB("ws5A", [128, D], BF16) as wsA5, \
                     SB("ws5B", [128, D], BF16) as wsB5, \
                     SB("l2st", [128, 32 * 256], BF16) as l2st:
                    for hf in range(2):
                        with SB("xo0", [128, D], F32) as xo0, \
                             SB("xo1", [128, D], F32) as xo1, \
                             SB("x10", [128, D], F32) as x10, \
                             SB("x11", [128, D], F32) as x11, \
                             SB("lb20", [128, D], BF16) as lb20, \
                             SB("lb21", [128, D], BF16) as lb21:
                            p.barrier_into(ENGS)
                            r0 = 512 * chunk + 256 * hf
                            ldx = {}
                            for tt, xo in ((0, xo0), (1, xo1)):
                                ldx[tt] = p.op("sync", lambda e, r0=r0, tt=tt, xo=xo:
                                               e.dma_start(out=xo[:],
                                                           in_=x_own[r0 + 128 * tt:
                                                                     r0 + 128 * (tt + 1), :]),
                                               inc="LD", k=16)
                            mm_end = {}
                            ev_end = {}
                            for oc in range(32):
                                buf = wsA5 if oc % 2 == 0 else wsB5
                                ps = psA if oc % 2 == 0 else psB
                                if oc >= 2:
                                    p.wait("sync", "MM", mm_end[oc - 2])
                                ldv = p.op("sync", lambda e, buf=buf, oc=oc: e.dma_start(
                                    out=buf[:],
                                    in_=wT_scr[128 * (96 + oc):128 * (97 + oc), :]),
                                    inc="LD", k=16)
                                p.wait("tensor", "LD", ldv)
                                if oc >= 2:
                                    p.wait("tensor", "EV", ev_end[oc - 2])
                                for ds in range(32):
                                    p.op("tensor", lambda e, buf=buf, ds=ds, ps=ps, hf=hf:
                                         e.matmul(ps[:, 0:256],
                                                  buf[:, 128 * ds:128 * (ds + 1)],
                                                  fmB[:, 512 * ds + 256 * hf:
                                                      512 * ds + 256 * hf + 256],
                                                  start=(ds == 0), stop=(ds == 31)),
                                         inc="MM" if ds == 31 else None)
                                mm_end[oc] = p.v("MM")
                                p.wait("vector", "MM", mm_end[oc])
                                p.op("vector", lambda e, ps=ps, oc=oc:
                                     e.tensor_scalar(
                                         out=fsA[:, 0:256], in0=ps[:, 0:256],
                                         scalar1=bo_sb[:, oc:oc + 1],
                                         scalar2=None, op0=ALU.add),
                                     inc="EV")
                                ev_end[oc] = dve_drain()
                                for tt, xo, x1 in ((0, xo0, x10), (1, xo1, x11)):
                                    land, free_ev = p.fland[p.flandi % 2]
                                    p.wait("tensor", "EV", max(free_ev, ev_end[oc]))
                                    p.op("tensor", lambda e, land=land, tt=tt:
                                         e.transpose(land[:, 0:128],
                                                     fsA[:, 128 * tt:128 * (tt + 1)],
                                                     identf_sb[:]), inc="MM")
                                    mmv = p.v("MM")
                                    p.wait("vector", "MM", mmv)
                                    p.wait("vector", "LD", ldx[tt])
                                    p.op("vector", lambda e, land=land, x1=x1, xo=xo, oc=oc:
                                         e.tensor_tensor(
                                             out=x1[:, 128 * oc:128 * (oc + 1)],
                                             in0=land[:, 0:128],
                                             in1=xo[:, 128 * oc:128 * (oc + 1)],
                                             op=ALU.add), inc="EV")
                                    p.fland[p.flandi % 2][1] = p.v("EV")
                                    p.flandi += 1
                                ev_end[oc] = p.v("EV")
                                mm_end[oc] = p.v("MM")
                            gv = p.op("vector", lambda e: e.tensor_copy(
                                lnst[:, 15:16], lnst[:, 14:15]), inc="EV")
                            p.wait("vector", "EV", gv)
                            gv2 = p.op("vector", lambda e: e.tensor_copy(
                                lnst[:, 15:16], lnst[:, 14:15]), inc="EV")
                            p.wait("scalar", "EV", gv2)
                            ev_n = ln_tiles([(x10, 0), (x11, 1)], [lb20, lb21], 0)
                            for tt, lb in ((0, lb20), (1, lb21)):
                                p.wait("tensor", "EV", ev_n)
                                tr_batch([(lb, 128 * ds) for ds in range(32)],
                                         [(l2st, 256 * ds + 128 * tt)
                                          for ds in range(32)])
                            ev_tr = dve_drain()
                            p.wait("scalar", "EV", ev_n)
                            for tt, x1 in ((0, x10), (1, x11)):
                                p.op("scalar", lambda e, r0=r0, tt=tt, x1=x1: e.dma_start(
                                    out=x1_scr[r0 + 128 * tt:r0 + 128 * (tt + 1), :],
                                    in_=x1[:]), inc="ST", k=16)
                            p.wait("scalar", "EV", ev_tr)
                            for ds in range(32):
                                p.op("scalar", lambda e, ds=ds, hf=hf: e.dma_start(
                                    out=ln2_scr[128 * ds:128 * (ds + 1),
                                                256 * hf:256 * (hf + 1)],
                                    in_=l2st[:, 256 * ds:256 * (ds + 1)]),
                                    inc="ST", k=16)
                            p.barrier_into(ENGS)

            g_fmB.__exit__(None, None, None)

            if NPH < 6:
                continue
            with SB("hfm", [128, 88 * 512], BF16) as hfm:
              with SB("fmL", [128, 32 * 512], BF16) as fmL, \
                 SB("gb0", [128, 8 * 512], BF16) as gb0, \
                 SB("gb1", [128, 8 * 512], BF16) as gb1, \
                 SB("ub0", [128, 8 * 512], BF16) as ub0, \
                 SB("ub1", [128, 8 * 512], BF16) as ub1:
                p.barrier_into(("sync",))
                ln2_src = bass.AP(ln2_scr.tensor, 0,
                                  [[512, 128], [512 * 128, 32], [1, 512]])
                ldl = p.op("sync", lambda e, s=ln2_src: e.dma_start(out=fmL[:], in_=s),
                           inc="LD", k=16)
                p.barrier_into(ENGS)

                ev_h = {}
                ev_sub = {}
                mm_h = {}
                sc_h = {}
                for hoc in range(88):
                    k = hoc // 11
                    hl = hoc % 11
                    gb = gb0 if hoc % 2 == 0 else gb1
                    ub = ub0 if hoc % 2 == 0 else ub1
                    if hoc >= 2:
                        p.wait("sync", "MM", mm_h[hoc - 2])
                    g_src = bass.AP(G_scr.tensor, (hl * 128) * D8,
                                    [[D8, 128], [11 * 128 * D8, 8], [1, D8]])
                    u_src = bass.AP(U_scr.tensor, (hl * 128) * D8,
                                    [[D8, 128], [11 * 128 * D8, 8], [1, D8]])
                    p.op("sync", lambda e, gb=gb, g_src=g_src: e.dma_start(
                        out=gb[:], in_=g_src), inc="LD", k=16)
                    ld_all = p.op("sync", lambda e, ub=ub, u_src=u_src: e.dma_start(
                        out=ub[:], in_=u_src), inc="LD", k=16)
                    ld_i = {i: ld_all for i in range(8)}
                    sgn = [SMAP[i, k] for i in range(8)]
                    first = {1.0: None, -1.0: None}
                    last = {1.0: None, -1.0: None}
                    for i in range(8):
                        if first[sgn[i]] is None:
                            first[sgn[i]] = i
                        last[sgn[i]] = i
                    if hoc >= 1:
                        p.wait("tensor", "EV", ev_sub[hoc - 1])
                    p.wait("tensor", "LD", ld_all)
                    for i in range(8):
                        j = int(JMAP[i, k])
                        gps = psA if sgn[i] > 0 else psB
                        ups = psC if sgn[i] > 0 else psD
                        st_ = (first[sgn[i]] == i)
                        sp_ = (last[sgn[i]] == i)
                        for ds in range(4):
                            co = 512 * j + 128 * ds
                            p.op("tensor", lambda e, gb=gb, co=co, ds=ds, gps=gps, i=i,
                                 st_=st_, sp_=sp_: e.matmul(
                                     gps[:], gb[:, co:co + 128],
                                     fmL[:, 512 * (4 * i + ds):512 * (4 * i + ds) + 512],
                                     start=(st_ and ds == 0), stop=(sp_ and ds == 3)),
                                 inc=None)
                            p.op("tensor", lambda e, ub=ub, co=co, ds=ds, ups=ups, i=i,
                                 st_=st_, sp_=sp_: e.matmul(
                                     ups[:], ub[:, co:co + 128],
                                     fmL[:, 512 * (4 * i + ds):512 * (4 * i + ds) + 512],
                                     start=(st_ and ds == 0), stop=(sp_ and ds == 3)),
                                 inc="MM" if (i == 7 and ds == 3) else None)
                    mm_h[hoc] = p.v("MM")
                    p.wait("scalar", "MM", mm_h[hoc])
                    if hoc >= 1:
                        p.wait("scalar", "EV", ev_h[hoc - 1])
                    p.op("scalar", lambda e: e.activation(
                        out=fsA[:], in_=psB[:], func=AF.Copy, bias=0.0, scale=-1.0))
                    scn = p.op("scalar", lambda e: e.activation(
                        out=fsB[:], in_=psD[:], func=AF.Copy, bias=0.0, scale=-1.0),
                        inc="SC")
                    p.wait("vector", "MM", mm_h[hoc])
                    p.wait("vector", "SC", scn)
                    if hoc >= 1:
                        p.wait("vector", "EV", ev_h[hoc - 1])
                    p.op("vector", lambda e: e.tensor_add(fsC[:], psA[:], fsA[:]))
                    ev_sub[hoc] = p.op("vector", lambda e: e.tensor_add(
                        fsD[:], psC[:], fsB[:]), inc="EV")
                    p.wait("scalar", "EV", ev_sub[hoc])
                    sc_h[hoc] = p.op("scalar", lambda e: e.activation(
                        out=fsA[:], in_=fsC[:], func=AF.Silu,
                        scale=thr_sb[:, 6:7]), inc="SC")
                    p.wait("vector", "SC", sc_h[hoc])
                    p.op("vector", lambda e, hoc=hoc: e.scalar_tensor_tensor(
                        out=hfm[:, 512 * hoc:512 * (hoc + 1)], in0=fsD[:],
                        scalar=thr_sb[:, 9:10], in1=fsA[:],
                        op0=ALU.mult, op1=ALU.mult), inc="EV")
                    ev_h[hoc] = dve_drain() if hoc == 87 else p.v("EV")

              if NPH >= 7:
                with SB("db0", [128, 8 * H8P], BF16) as db0, \
                     SB("db1", [128, 8 * H8P], BF16) as db1, \
                     SB("x1r0", [128, D], F32) as x1r0, \
                     SB("x1r1", [128, D], F32) as x1r1, \
                     SB("ob0", [128, 128], F32) as ob0, \
                     SB("ob1", [128, 128], F32) as ob1:
                    for hf in range(2):
                        p.barrier_into(ENGS)
                        r0 = 512 * chunk + 256 * hf
                        ldx = {}
                        for tt, x1r in ((0, x1r0), (1, x1r1)):
                            ldx[tt] = p.op("sync", lambda e, r0=r0, tt=tt, x1r=x1r:
                                           e.dma_start(out=x1r[:],
                                                       in_=x1_scr[r0 + 128 * tt:
                                                                  r0 + 128 * (tt + 1), :]),
                                           inc="LD", k=16)
                        mm_end = {}
                        ev_end = {}
                        st_end = {}
                        for oc in range(32):
                            kd = oc // 4
                            ocl = oc % 4
                            db_ = db0 if oc % 2 == 0 else db1
                            psP = psA if oc % 2 == 0 else psC
                            psN = psB if oc % 2 == 0 else psD
                            if oc >= 2:
                                p.wait("sync", "MM", mm_end[oc - 2])
                            d_src = bass.AP(Dn_scr.tensor, (ocl * 128) * H8P,
                                            [[H8P, 128], [4 * 128 * H8P, 8], [1, H8P]])
                            ld_all = p.op("sync", lambda e, db_=db_, d_src=d_src:
                                          e.dma_start(out=db_[:], in_=d_src),
                                          inc="LD", k=16)
                            sgn = [SMAP[ih, kd] for ih in range(8)]
                            first = {1.0: None, -1.0: None}
                            last = {1.0: None, -1.0: None}
                            for ih in range(8):
                                if first[sgn[ih]] is None:
                                    first[sgn[ih]] = ih
                                last[sgn[ih]] = ih
                            if oc >= 2:
                                p.wait("tensor", "EV", ev_end[oc - 2])
                            p.wait("tensor", "LD", ld_all)
                            for ih in range(8):
                                j = int(JMAP[ih, kd])
                                ps = psP if sgn[ih] > 0 else psN
                                st_ = (first[sgn[ih]] == ih)
                                sp_ = (last[sgn[ih]] == ih)
                                for hs in range(11):
                                    c = 512 * (11 * ih + hs) + 256 * hf
                                    co = H8P * j + 128 * hs
                                    p.op("tensor", lambda e, db_=db_, co=co, ps=ps,
                                         c=c, st_=st_, sp_=sp_, hs=hs: e.matmul(
                                             ps[:, 0:256],
                                             db_[:, co:co + 128],
                                             hfm[:, c:c + 256],
                                             start=(st_ and hs == 0),
                                             stop=(sp_ and hs == 10)),
                                         inc="MM" if (ih == 7 and hs == 10) else None)
                            mm_end[oc] = p.v("MM")
                            p.wait("scalar", "MM", mm_end[oc])
                            if oc >= 1:
                                p.wait("scalar", "EV", ev_end[oc - 1])
                            scn = p.op("scalar", lambda e, psN=psN: e.activation(
                                out=fsB[:, 0:256], in_=psN[:, 0:256],
                                func=AF.Copy, bias=0.0, scale=-1.0), inc="SC")
                            p.wait("vector", "MM", mm_end[oc])
                            p.wait("vector", "SC", scn)
                            p.op("vector", lambda e, psP=psP: e.tensor_add(
                                fsA[:, 0:256], psP[:, 0:256], fsB[:, 0:256]),
                                inc="EV")
                            ev_end[oc] = dve_drain()
                            for tt, x1r, ob in ((0, x1r0, ob0), (1, x1r1, ob1)):
                                land, free_ev = p.fland[p.flandi % 2]
                                p.wait("tensor", "EV", max(free_ev, ev_end[oc]))
                                p.op("tensor", lambda e, land=land, tt=tt: e.transpose(
                                    land[:, 0:128], fsA[:, 128 * tt:128 * (tt + 1)],
                                    identf_sb[:]), inc="MM")
                                mmv = p.v("MM")
                                p.wait("vector", "MM", mmv)
                                p.wait("vector", "LD", ldx[tt])
                                key = (oc - 1, tt)
                                if key in st_end:
                                    p.wait("vector", "ST", st_end[key])
                                p.op("vector", lambda e, land=land, x1r=x1r, oc=oc, ob=ob:
                                     e.tensor_tensor(out=ob[:], in0=land[:, 0:128],
                                                     in1=x1r[:, 128 * oc:128 * (oc + 1)],
                                                     op=ALU.add), inc="EV")
                                p.fland[p.flandi % 2][1] = p.v("EV")
                                p.flandi += 1
                                evv = dve_drain()
                                p.wait("scalar", "EV", evv)
                                st_end[(oc, tt)] = p.op(
                                    "scalar", lambda e, r0=r0, tt=tt, oc=oc, ob=ob:
                                    e.dma_start(
                                        out=y_out[r0 + 128 * tt:r0 + 128 * (tt + 1),
                                                  128 * oc:128 * (oc + 1)],
                                        in_=ob[:]), inc="ST", k=16)
                            ev_end[oc] = p.v("EV")
                            mm_end[oc] = p.v("MM")

        p.wait("scalar", "ST", p.v("ST"))
        if DEBUG:
            dbgs = [("dbg_K", K_scr, [H * 128, SEQ], BF16),
                    ("dbg_V", V_scr, [H * 128, SEQ], BF16),
                    ("dbg_Q", Q_scr, [H * 128, NQ], BF16),
                    ("dbg_x1", x1_scr, [NQ, D], F32),
                    ("dbg_ln2", ln2_scr, [D, 512], BF16),
                    ("dbg_G", G_scr, [HIDP, D8], BF16),
                    ("dbg_D", Dn_scr, [8 * D8, H8P], BF16),
                    ("dbg_wT", wT_scr, [128 * 32, D], BF16)]
            for nm, scr, shape, dt in dbgs:
                dbg = nc.dram_tensor(nm, shape, dt, kind="ExternalOutput").ap()
                p.op("scalar", lambda e, dbg=dbg, scr=scr, shape=shape: e.dma_start(
                    out=dbg[:, :], in_=scr[0:shape[0], :]), inc="ST", k=16)
            dbg_thr = nc.dram_tensor("dbg_thr", [128, 12], F32,
                                     kind="ExternalOutput").ap()
            p.op("scalar", lambda e: e.dma_start(out=dbg_thr[:, :], in_=thr_sb[:]),
                 inc="ST", k=16)
            p.wait("scalar", "ST", p.v("ST"))

        with nc.Block() as blk:
            @blk.tensor
            def _(e):
                for f in p.q["tensor"]:
                    f(e)

            @blk.vector
            def _(e):
                for f in p.q["vector"]:
                    f(e)

            @blk.scalar
            def _(e):
                for f in p.q["scalar"]:
                    f(e)

            @blk.sync
            def _(e):
                for f in p.q["sync"]:
                    f(e)

            @blk.gpsimd
            def _(e):
                for f in p.q["gpsimd"]:
                    f(e)

    return nc


_NC_CACHE = None


def _get_nc():
    global _NC_CACHE
    if _NC_CACHE is None:
        _NC_CACHE = build_program()
    return _NC_CACHE


def kernel(L0_attn_x, L0_attn_w_qkv, L0_attn_b_qkv, L0_attn_w_out, L0_attn_b_out,
           L0_ffn_w_gate, L0_ffn_w_up, L0_ffn_w_down):
    x = np.ascontiguousarray(np.asarray(L0_attn_x, dtype=np.float32))
    w_qkv = np.ascontiguousarray(np.asarray(L0_attn_w_qkv, dtype=np.float32))
    b_qkv = np.asarray(L0_attn_b_qkv, dtype=np.float32)
    w_out = np.ascontiguousarray(np.asarray(L0_attn_w_out, dtype=np.float32))
    b_out = np.asarray(L0_attn_b_out, dtype=np.float32)
    wg = np.asarray(L0_ffn_w_gate, dtype=np.float32)
    wu = np.asarray(L0_ffn_w_up, dtype=np.float32)
    wd = np.asarray(L0_ffn_w_down, dtype=np.float32)

    bq_t = np.ascontiguousarray(b_qkv.reshape(96, 128).T)
    bo_t = np.ascontiguousarray(b_out.reshape(32, 128).T)
    wg_p = np.zeros((8, H8P, D8), np.float32)
    wg_p[:, :H8, :] = wg
    wg_p = wg_p.reshape(HIDP, D8)
    wu_p = np.zeros((8, H8P, D8), np.float32)
    wu_p[:, :H8, :] = wu
    wu_p = wu_p.reshape(HIDP, D8)
    wd_p = np.zeros((8, D8, H8P), np.float32)
    wd_p[:, :, :H8] = wd
    wd_p = wd_p.reshape(8 * D8, H8P)

    negk9 = (-1e9 * np.arange(SEQ, dtype=np.float64)).astype(np.float32).reshape(1, SEQ)
    pos = (np.arange(SEQ, dtype=np.float32) - (SEQ - 1)).reshape(1, SEQ)
    n32 = 32
    m = np.arange(1, n32 + 1, dtype=np.float32) * (8.0 / n32)
    slopes = (1.0 / np.power(2.0, m)).astype(np.float32)
    slopes_rep = np.ascontiguousarray(np.repeat(slopes.reshape(1, 32), 128, axis=0))

    nc = _get_nc()
    in_maps = []
    for c in range(8):
        b = c // 4
        q0 = 1024 * (c % 4)
        qbig = (1e9 * (q0 + np.arange(128).reshape(128, 1)
                       + 128 * np.arange(8).reshape(1, 8) + 0.5)).astype(np.float32)
        in_maps.append({
            "x_own": np.ascontiguousarray(x[b, q0:q0 + NQ]),
            "x_bat": x[b],
            "w_qkv": w_qkv,
            "w_out": w_out,
            "bq_t": bq_t,
            "bo_t": bo_t,
            "wg_p": wg_p,
            "wu_p": wu_p,
            "wd_p": wd_p,
            "qbig": qbig,
            "negk9": negk9,
            "pos": pos,
            "slopes": slopes_rep,
        })
    res = run_bass_kernel_spmd(nc, in_maps, core_ids=list(range(8)))
    out = np.empty((B, SEQ, D), np.float32)
    for c in range(8):
        b = c // 4
        q0 = 1024 * (c % 4)
        out[b, q0:q0 + NQ] = res.results[c]["y"]
    return out
